# version 31
# baseline (speedup 1.0000x reference)
import contextlib
import numpy as np
import concourse.bass as bass
import concourse.mybir as mybir
from concourse.bass_utils import run_bass_kernel_spmd

F32 = mybir.dt.float32
BF16 = mybir.dt.bfloat16
F16 = mybir.dt.float16
I8 = mybir.dt.int8
ALU = mybir.AluOpType
ACT = mybir.ActivationFunctionType
AXX = mybir.AxisListType.X
P = 128
NEG = -1.0e30
RMAGIC = 12582912.0


def _flat(t):
    return int(np.prod(t.shape[1:]))


def ap2(t, off, n, pcnt=P):
    return bass.AP(t, off, [[_flat(t), pcnt], [1, n]])


def ap3(t, off, d1, s1, d2, s2=1, pcnt=P):
    return bass.AP(t, off, [[_flat(t), pcnt], [s1, d1], [s2, d2]])


class Eng:
    def __init__(self, nc, ctx, eng, name):
        self.e = eng
        self.sem = ctx.enter_context(nc.semaphore(name))
        self.c = 0

    def inc(self, inst):
        self.c += 1
        inst.then_inc(self.sem, 1)
        return self.c

    def wait(self, other, cnt=None):
        c = other.c if cnt is None else cnt
        if c > 0:
            self.e.wait_ge(other.sem, c)


class DmaQ:

    def __init__(self, nc, ctx, name, eng, k=3):
        self.eng = eng
        self.sems = [ctx.enter_context(nc.semaphore(f"{name}{i}")) for i in range(k)]
        self.cnt = [0] * k
        self.n = 0

    def put(self, out, in_, transpose=False, slow=False):
        i = self.n % len(self.sems)
        if self.cnt[i]:
            self.eng.wait_ge(self.sems[i], self.cnt[i] * 16)
        inst = (self.eng.dma_start_transpose(out, in_) if transpose
                else self.eng.dma_start(out, in_, allow_slow_non_contiguous=slow))
        inst.then_inc(self.sems[i], 16)
        self.cnt[i] += 1
        self.n += 1
        return (self.sems[i], self.cnt[i] * 16)

    def wait(self, engine, tok):
        e = engine.e if isinstance(engine, Eng) else engine
        e.wait_ge(tok[0], tok[1])

    def wait_all(self, engine):
        e = engine.e if isinstance(engine, Eng) else engine
        for s, c in zip(self.sems, self.cnt):
            if c:
                e.wait_ge(s, c * 16)


def build_nc(T=4096, C=1024, II=4096, TQ=1024, TTILE=512, ACH=256, LWc=128):
    NCT = C // P
    NIT = II // P
    NQ = T // TQ
    NTT = TQ // TTILE
    NSUB = TTILE // P
    IW = min(512, II)
    NIW = II // IW
    CW = min(512, C)
    NCW = C // CW
    NACH = TQ // ACH
    NLW = ACH // LWc
    TQP = TQ + 16

    nc = bass.Bass("TRN2", target_bir_lowering=False)

    x_in = nc.dram_tensor("x", [T, C], F32, kind="ExternalInput").ap()
    out = nc.dram_tensor("out", [T, C], F32, kind="ExternalOutput").ap()
    wk_in = nc.dram_tensor("Wk", [C, C], F32, kind="ExternalInput").ap()
    wv_in = nc.dram_tensor("Wv", [C, C], F32, kind="ExternalInput").ap()
    wr_in = nc.dram_tensor("Wr", [C, C], F32, kind="ExternalInput").ap()
    wo_in = nc.dram_tensor("Wo", [C, C], F32, kind="ExternalInput").ap()
    wg_in = nc.dram_tensor("Wg", [II, C], F32, kind="ExternalInput").ap()
    wu_in = nc.dram_tensor("Wu", [II, C], F32, kind="ExternalInput").ap()
    wd_in = nc.dram_tensor("Wd", [C, II], F32, kind="ExternalInput").ap()
    p_in = [nc.dram_tensor(nm, [C], F32, kind="ExternalInput").ap()
            for nm in ("pw", "pu", "pmk", "pmv", "pmr")]

    scr_g = nc.dram_tensor("scr_g", [II, C], BF16, kind="Internal").ap()
    scr_u = nc.dram_tensor("scr_u", [II, C], BF16, kind="Internal").ap()
    scr_d = nc.dram_tensor("scr_d", [C, II], BF16, kind="Internal").ap()
    scr_gT = nc.dram_tensor("scr_gT", [C, II], BF16, kind="Internal").ap()
    scr_uT = nc.dram_tensor("scr_uT", [C, II], BF16, kind="Internal").ap()
    scr_dT = nc.dram_tensor("scr_dT", [II, C], BF16, kind="Internal").ap()
    scr_w4 = [nc.dram_tensor(f"scr_w4_{i}", [C, C], BF16, kind="Internal").ap()
              for i in range(4)]
    scr_xq = [nc.dram_tensor(f"scr_xq{i}", [TTILE, C], BF16, kind="Internal").ap()
              for i in range(2)]
    scr_hq = [nc.dram_tensor(f"scr_hq{i}", [TTILE, II], BF16, kind="Internal").ap()
              for i in range(2)]
    scr_xn = nc.dram_tensor("scr_xn", [TQ, C], BF16, kind="Internal").ap()

    ctx = contextlib.ExitStack()
    with ctx:
        def sbt(name, shape, dt=F32):
            return ctx.enter_context(nc.sbuf_tensor(name, shape, dt))

        TE = Eng(nc, ctx, nc.tensor, "eT")
        V = Eng(nc, ctx, nc.vector, "eV")
        S = Eng(nc, ctx, nc.scalar, "eS")
        G = Eng(nc, ctx, nc.gpsimd, "eG")
        qx = DmaQ(nc, ctx, "qx", nc.sync, k=5)
        qtern = DmaQ(nc, ctx, "qt", nc.sync, k=4)
        qtr = DmaQ(nc, ctx, "qr", nc.sync, k=4)
        qst = DmaQ(nc, ctx, "qs", nc.gpsimd, k=4)
        qso = DmaQ(nc, ctx, "qo", nc.scalar, k=4)
        ALLQ = [qx, qtern, qtr, qst, qso]

        def barrier():
            for a in (TE, V, S, G):
                for b in (TE, V, S, G):
                    if a is not b:
                        a.wait(b)
                for q_ in ALLQ:
                    q_.wait_all(a)
            for b in (TE, V, S, G):
                if b.c:
                    nc.sync.wait_ge(b.sem, b.c)

        psA = ctx.enter_context(nc.psum_tensor("psA", [P, 8, 512], F32))

        def psum(bank, n, pcnt=P):
            return ap2(psA, bank * 512, n, pcnt)

        xres = sbt("xres", [P, TQ // P, C])
        wexp = sbt("wexp", [P, NCT])
        lwexp = sbt("lwexp", [P, NCT])
        cstw = sbt("cstw", [P, NCT])
        mixk = sbt("mixk", [P, NCT])
        mixv = sbt("mixv", [P, NCT])
        mixr = sbt("mixr", [P, NCT])
        ramp = sbt("ramp", [P, NCT, LWc])
        stO = sbt("stO", [P, NCT])
        stP = sbt("stP", [P, NCT])
        stQ = sbt("stQ", [P, NCT])
        onescol = sbt("onescol", [P, 1])
        onesrow = sbt("onesrow", [1, P])
        wsc = sbt("wsc", [P, 6])
        xnbound = sbt("xnbound", [P, NCT], BF16)

        params_raw = sbt("params_raw", [P, NCT * 5])
        iota = sbt("iota", [P, LWc])
        ptoks = []
        for i, pin in enumerate(p_in):
            src = bass.AP(pin.tensor, 0, [[1, P], [P, NCT]])
            ptoks.append(qx.put(ap2(params_raw, i * NCT, NCT), src, slow=True))
        for t_ in ptoks:
            qx.wait(V, t_)
            qx.wait(S, t_)
        S.inc(nc.scalar.activation(wexp[:], ap2(params_raw, 0, NCT), ACT.Exp))
        V.wait(S)
        V.inc(nc.vector.tensor_sub(cstw[:], wexp[:], ap2(params_raw, NCT, NCT)))
        V.inc(nc.vector.tensor_scalar_mul(lwexp[:], wexp[:], float(LWc)))
        V.inc(nc.vector.tensor_copy(mixk[:], ap2(params_raw, 2 * NCT, NCT)))
        V.inc(nc.vector.tensor_copy(mixv[:], ap2(params_raw, 3 * NCT, NCT)))
        V.inc(nc.vector.tensor_copy(mixr[:], ap2(params_raw, 4 * NCT, NCT)))
        V.inc(nc.vector.memset(onescol[:], 1.0))
        V.inc(nc.vector.memset(onesrow[:], 1.0))
        V.inc(nc.vector.memset(iota[:], 1.0))
        V.inc(nc.vector.memset(stO[:], NEG))
        V.inc(nc.vector.memset(stP[:], 0.0))
        V.inc(nc.vector.memset(stQ[:], 0.0))
        V.inc(nc.vector.memset(xnbound[:], 0.0))
        V.wait(V)
        V.inc(nc.vector.tensor_tensor_scan(iota[:], iota[:], iota[:], 0.0,
                                           ALU.mult, ALU.add))
        V.wait(V)
        iota_b = bass.AP(iota, 0, [[LWc, P], [0, NCT], [1, LWc]])
        wexp_b = bass.AP(wexp, 0, [[NCT, P], [1, NCT], [0, LWc]])
        V.inc(nc.vector.tensor_tensor(ramp[:], iota_b, wexp_b, ALU.mult))

        w4_toks = [qst.put(scr_w4[wi][:, :], win[:, :])
                   for wi, win in enumerate((wk_in, wv_in, wr_in, wo_in))]

        x0src = bass.AP(x_in.tensor, 0, [[C, P], [P * C, TQ // P], [1, C]])
        tx0 = qx.put(xres[:], x0src)

        WMAX = max(C, II)
        BELEM = 4096
        sctx = contextlib.ExitStack()
        def sbt0(name, shape, dt=F32):
            return sctx.enter_context(nc.sbuf_tensor(name, shape, dt))
        wbat = [sbt0(f"wbat{i}", [P, BELEM]) for i in range(4)]
        qbat = sbt0("qbat", [P, BELEM])
        ibat = [sbt0(f"ibat{i}", [P, BELEM], I8) for i in range(2)]
        bbat = [sbt0(f"bbat{i}", [P, BELEM], BF16) for i in range(2)]
        mbat = [sbt0(f"mbat{i}", [P, BELEM]) for i in range(2)]
        obat = sbt0("obat", [P, BELEM])
        acc = sbt0("acc", [P, 1])
        red = sbt0("red", [P, 24])
        rowbuf = sbt0("rowbuf", [1, P])
        sca = sbt0("sca", [1, 2])
        trow = [sbt0(f"trow{i}", [P, WMAX], BF16) for i in range(2)]
        qg = DmaQ(nc, ctx, "qg", nc.sync, k=4)
        qm = DmaQ(nc, ctx, "qm", nc.scalar, k=4)
        ALLQ.extend([qg, qm])
        scr_red = nc.dram_tensor("scr_red", [P], F32, kind="Internal").ap()
        scr_sca = nc.dram_tensor("scr_sca", [2], F32, kind="Internal").ap()

        wlist = [(0, wg_in, scr_g, scr_gT, II, C, True),
                 (1, wu_in, scr_u, scr_uT, II, C, True),
                 (2, wd_in, scr_d, scr_dT, C, II, False)]
        def nbat(RR, CC2):
            BW_ = min(BELEM // CC2, RR // P)
            return RR // P // BW_
        mean_sdone = {}
        mb_guard = [0, 0]
        wb_guard = [0, 0, 0, 0]
        ib_guard = [0, 0]
        bb_guard = [None, None]
        sc_prev = {"t1": None, "t2": None, "t3": None}

        def mean_batch(oi, win, RR, CC2, nb):
            BW_ = min(BELEM // CC2, RR // P)
            BN = BW_ * CC2
            sd = mean_sdone.setdefault(oi, {})
            b = nb % 2
            if mb_guard[b]:
                nc.scalar.wait_ge(S.sem, mb_guard[b])
            tok = qm.put(ap2(mbat[b], 0, BN),
                         win[nb * BW_ * P:(nb + 1) * BW_ * P, :])
            qm.wait(S, tok)
            S.wait(S)
            S.inc(nc.scalar.activation(
                ap2(obat, 0, BN), ap2(mbat[b], 0, BN), ACT.Abs,
                accum_out=red[:, 8 * oi + nb:8 * oi + nb + 1]))
            sd[nb] = S.c
            mb_guard[b] = S.c

        def scale_of(oi, widx, RR, CC2):
            NB = nbat(RR, CC2)
            V.wait(S, mean_sdone[oi][NB - 1])
            if sc_prev["t1"] is not None:
                qso.wait(V, sc_prev["t1"])
            V.inc(nc.vector.tensor_reduce(acc[:], red[:, 8 * oi:8 * oi + NB],
                                          AXX, ALU.add))
            V.wait(V)
            nc.scalar.wait_ge(V.sem, V.c)
            if sc_prev["t2"] is not None:
                qm.wait(nc.scalar, sc_prev["t2"])
            t1 = qso.put(bass.AP(scr_red.tensor, 0, [[1, P], [1, 1]]), acc[:])
            sc_prev["t1"] = t1
            qso.wait(nc.scalar, t1)
            t2 = qm.put(rowbuf[0:1, :],
                        bass.AP(scr_red.tensor, 0, [[P, 1], [1, P]]))
            sc_prev["t2"] = t2
            qm.wait(S, t2)
            if sc_prev["t3"] is not None:
                qso.wait(S, sc_prev["t3"])
            S.wait(S)
            S.inc(nc.scalar.activation(ap2(obat, 0, P, 1), rowbuf[0:1, :],
                                       ACT.Copy, accum_out=sca[0:1, 0:1],
                                       scale=1.0 / float(RR * CC2)))
            V.wait(S)
            V.inc(nc.vector.tensor_scalar_max(sca[:1, 0:1], sca[:1, 0:1], 1.0e-5))
            V.wait(V)
            V.inc(nc.vector.reciprocal(sca[:1, 1:2], sca[:1, 0:1]))
            V.wait(V)
            nc.scalar.wait_ge(V.sem, V.c)
            t3 = qso.put(bass.AP(scr_sca.tensor, 0, [[2, 1], [1, 2]]),
                         sca[0:1, 0:2])
            sc_prev["t3"] = t3
            qso.wait(nc.scalar, t3)
            t4 = qm.put(wsc[:, 2 * widx:2 * widx + 2],
                        bass.AP(scr_sca.tensor, 0, [[0, P], [1, 2]]), slow=True)
            qm.wait(V, t4)
            qm.wait(S, t4)

        prev_tr = [None, None]
        relayout_pend = []

        def relayout_block(scr_nat, scr_T, RR, cb):
            b = cb % 2
            if prev_tr[b] is not None:
                qst.wait(nc.sync, prev_tr[b])
            tok = qtr.put(ap2(trow[b], 0, RR),
                          scr_nat[:, cb * P:(cb + 1) * P], transpose=True)
            qtr.wait(nc.gpsimd, tok)
            prev_tr[b] = qst.put(scr_T[cb * P:(cb + 1) * P, :],
                                 ap2(trow[b], 0, RR))

        for oi, (widx, win, scr_nat, scr_T, RR, CC2, do_rl) in enumerate(wlist):
            NB = nbat(RR, CC2)
            if oi == 0:
                for nb in range(NB):
                    mean_batch(oi, win, RR, CC2, nb)
            nxt0 = wlist[oi + 1] if oi + 1 < len(wlist) else None
            NBn = nbat(nxt0[4], nxt0[5]) if nxt0 is not None else 0
            scale_of(oi, widx, RR, CC2)
            s_w_ap = wsc[:, 2 * widx + 1:2 * widx + 2]
            BW_ = min(BELEM // CC2, RR // P)
            BN = BW_ * CC2
            vdone = {}
            sdone2 = {}
            st_toks = {}
            pend = relayout_pend
            relayout_pend = []
            rl_per = (len(pend) + NB - 1) // NB if pend else 0
            for nb in range(NB):
                b = nb % 4
                if wb_guard[b]:
                    nc.sync.wait_ge(V.sem, wb_guard[b])
                tok = qg.put(ap2(wbat[b], 0, BN),
                             win[nb * BW_ * P:(nb + 1) * BW_ * P, :])
                qg.wait(V, tok)
                V.inc(nc.vector.tensor_scalar(ap2(qbat, 0, BN),
                                              ap2(wbat[b], 0, BN),
                                              s_w_ap, 1.49, ALU.mult, ALU.min))
                V.wait(V)
                V.inc(nc.vector.tensor_scalar(ap2(wbat[b], 0, BN),
                                              ap2(qbat, 0, BN),
                                              -1.49, RMAGIC, ALU.max, ALU.add))
                V.wait(V)
                b2 = nb % 2
                if ib_guard[b2]:
                    V.wait(S, ib_guard[b2])
                V.inc(nc.vector.tensor_scalar(ap2(ibat[b2], 0, BN),
                                              ap2(wbat[b], 0, BN),
                                              RMAGIC, None, ALU.subtract))
                vdone[nb] = V.c
                wb_guard[b] = V.c
                S.wait(V, vdone[nb])
                if bb_guard[b2] is not None:
                    qso.wait(S, bb_guard[b2])
                S.inc(nc.scalar.activation(ap2(bbat[b2], 0, BN),
                                           ap2(ibat[b2], 0, BN), ACT.Copy))
                sdone2[nb] = S.c
                ib_guard[b2] = S.c
                nc.scalar.wait_ge(S.sem, S.c)
                st_toks[nb] = qso.put(
                    scr_nat[nb * BW_ * P:(nb + 1) * BW_ * P, :],
                    ap2(bbat[b2], 0, BN))
                bb_guard[b2] = st_toks[nb]
                if nxt0 is not None and nb < NBn:
                    mean_batch(oi + 1, nxt0[1], nxt0[4], nxt0[5], nb)
                for _ in range(rl_per):
                    if pend:
                        pend.pop(0)()
            if nxt0 is not None:
                for nb2 in range(NB, NBn):
                    mean_batch(oi + 1, nxt0[1], nxt0[4], nxt0[5], nb2)
            while pend:
                pend.pop(0)()
            if do_rl:
                qso.wait_all(nc.sync)
                relayout_pend = [
                    (lambda sn=scr_nat, st=scr_T, rr=RR, cb_=cb:
                     relayout_block(sn, st, rr, cb_))
                    for cb in range(CC2 // P)]
        while relayout_pend:
            relayout_pend.pop(0)()
        barrier()
        sctx.close()

        lnm = sbt("lnm", [P, 8])
        lnv = sbt("lnv", [P, 8])
        lnr = sbt("lnr", [P, 8])
        lnt = sbt("lnt", [P, 8])
        ln_a = sbt("ln_a", [P, 8])
        ln_b = sbt("ln_b", [P, 8])
        sA = sbt("sA", [P, NSUB])
        dsc = sbt("dsc", [P, NSUB])
        invA = sbt("invA", [P, NSUB])
        gsc = sbt("gsc", [P, NSUB])
        usc = sbt("usc", [P, NSUB])
        s2v = sbt("s2v", [P, NSUB])
        phase_n = [0]

        LW1 = LWc + 1

        def ln_tile(src_t, src_off, ns, dst_t, dst_off, sq_t):
            x3 = ap3(src_t, src_off, ns, C, C)
            d3 = ap3(dst_t, dst_off, ns, C, C)
            S.wait(V)
            V.inc(nc.vector.tensor_reduce(lnm[:, 0:ns], x3, AXX, ALU.add))
            for j in range(ns):
                S.inc(nc.scalar.activation(ap2(sq_t, j * C, C),
                                           ap2(src_t, src_off + j * C, C),
                                           ACT.Square, accum_out=lnv[:, j:j + 1]))
            V.wait(V)
            V.inc(nc.vector.tensor_scalar(lnm[:, 0:ns], lnm[:, 0:ns], 1.0 / C,
                                          None, ALU.mult))
            V.wait(V)
            V.inc(nc.vector.tensor_tensor(lnt[:, 0:ns], lnm[:, 0:ns], lnm[:, 0:ns],
                                          ALU.mult))
            V.wait(S)
            V.inc(nc.vector.tensor_scalar(lnv[:, 0:ns], lnv[:, 0:ns], 1.0 / C,
                                          1.0e-5, ALU.mult, ALU.add))
            V.wait(V)
            V.inc(nc.vector.tensor_sub(lnv[:, 0:ns], lnv[:, 0:ns], lnt[:, 0:ns]))
            V.wait(V)
            V.inc(nc.vector.reciprocal(ln_a[:, 0:ns], lnv[:, 0:ns]))
            S.wait(V)
            S.inc(nc.scalar.activation(lnt[:, 0:ns], ln_a[:, 0:ns], ACT.Sqrt))
            V.wait(S)
            V.inc(nc.vector.tensor_tensor(ln_b[:, 0:ns], lnt[:, 0:ns], lnt[:, 0:ns],
                                          ALU.mult))
            V.wait(V)
            V.inc(nc.vector.tensor_tensor(ln_b[:, 0:ns], lnv[:, 0:ns], ln_b[:, 0:ns],
                                          ALU.mult))
            V.wait(V)
            V.inc(nc.vector.tensor_scalar(ln_a[:, 0:ns], ln_b[:, 0:ns], -0.5, 1.5,
                                          ALU.mult, ALU.add))
            V.wait(V)
            V.inc(nc.vector.tensor_tensor(lnr[:, 0:ns], lnt[:, 0:ns], ln_a[:, 0:ns],
                                          ALU.mult))
            V.wait(V)
            m_bc = bass.AP(lnm, 0, [[8, P], [1, ns], [0, C]])
            r_bc = bass.AP(lnr, 0, [[8, P], [1, ns], [0, C]])
            V.inc(nc.vector.tensor_tensor(d3, x3, m_bc, ALU.subtract))
            V.wait(V)
            V.inc(nc.vector.tensor_tensor(d3, d3, r_bc, ALU.mult))
            V.wait(V)

        state = {"xqT_te": {}, "xq_war": [None, None], "hq_war": [None, None]}

        def ffn(q):
            phase_n[0] += 1
            pn = phase_n[0]
            fctx = contextlib.ExitStack()
            def fb(name, shape, dt=F32):
                return fctx.enter_context(nc.sbuf_tensor(f"{name}_{pn}", shape, dt))
            xqT = [fb(f"xqT{i}", [P, NCT, TTILE], BF16) for i in range(2)]
            gate_w = [fb(f"gate_w{i}", [P, IW]) for i in range(2)]
            u_w = [fb(f"u_w{i}", [P, IW]) for i in range(2)]
            hbuf = fb("hbuf", [P, NSUB, II], F16)
            hqb = fb("hqb", [P, II], BF16)
            xnt_ = fb("xnt_", [P, NSUB, C])
            xqb3 = fb("xqb3", [P, NSUB, C], BF16)
            xqi3 = fb("xqi3", [P, NSUB, C], I8)
            hqi = fb("hqi", [P, II], I8)
            h05b = xnt_ if NSUB * C >= II else fb("h05b", [P, II])
            gslab = [fb(f"gslab{i}", [P, NCT, IW], BF16) for i in range(2)]
            uslab = [fb(f"uslab{i}", [P, NCT, IW], BF16) for i in range(2)]
            dslab = [fb(f"dslab{i}", [P, C], BF16) for i in range(2)]
            hqT = [fb(f"hqT{i}", [P, TTILE], BF16) for i in range(2)]
            gsc2 = [fb(f"gsc2{i}", [P, NSUB]) for i in range(2)]
            usc2 = [fb(f"usc2{i}", [P, NSUB]) for i in range(2)]
            st = {"xqT_toks": {}, "hq_toks": {}, "rounds_v": {}, "gu_te": {},
                  "dn_evac": 0, "bank_evac": {}, "buf_free": {}}

            def quant(tt):
                alt = tt % 2
                sxq = scr_xq[alt]
                if state["xq_war"][alt] is not None:
                    qtr.wait(nc.scalar, state["xq_war"][alt])
                if state.get("xqb_tok") is not None:
                    qso.wait(V, state["xqb_tok"])
                    qso.wait(S, state["xqb_tok"])
                ln_tile(xres, tt * NSUB * C, NSUB, xnt_, 0, xqb3)
                V.inc(nc.vector.tensor_reduce(sA[:, 0:NSUB], xnt_[:], AXX,
                                              ALU.max, apply_absolute_value=True))
                V.wait(V)
                V.inc(nc.vector.tensor_scalar(ln_a[:, 0:NSUB], sA[:, 0:NSUB],
                                              1.0e-5, None, ALU.max))
                V.wait(V)
                V.inc(nc.vector.tensor_scalar(invA[:, 0:NSUB], ln_a[:, 0:NSUB],
                                              1.0 / 127.0, None, ALU.mult))
                V.inc(nc.vector.reciprocal(ln_b[:, 0:NSUB], ln_a[:, 0:NSUB]))
                V.wait(V)
                V.inc(nc.vector.tensor_scalar(sA[:, 0:NSUB], ln_b[:, 0:NSUB],
                                              127.0, None, ALU.mult))
                V.inc(nc.vector.tensor_tensor(gsc2[alt][:], invA[:, 0:NSUB],
                                              bass.AP(wsc, 0, [[6, P], [0, NSUB]]),
                                              ALU.mult))
                V.inc(nc.vector.tensor_tensor(usc2[alt][:], invA[:, 0:NSUB],
                                              bass.AP(wsc, 2, [[6, P], [0, NSUB]]),
                                              ALU.mult))
                V.wait(V)
                sA_bc = bass.AP(sA, 0, [[NSUB, P], [1, NSUB], [0, C]])
                V.inc(nc.vector.tensor_tensor(xnt_[:], xnt_[:], sA_bc, ALU.mult))
                V.wait(V)
                V.inc(nc.vector.tensor_scalar(xqi3[:], xnt_[:], RMAGIC, RMAGIC,
                                              ALU.add, ALU.subtract))
                V.wait(V)
                V.inc(nc.vector.tensor_copy(xqb3[:], xqi3[:]))
                nc.scalar.wait_ge(V.sem, V.c)
                tok = qso.put(
                    bass.AP(sxq.tensor, 0, [[C, P], [P * C, NSUB], [1, C]]),
                    xqb3[:])
                state["xqb_tok"] = tok
                if state["xqT_te"].get(alt):
                    nc.sync.wait_ge(TE.sem, state["xqT_te"][alt])
                qso.wait(nc.sync, tok)
                toks = []
                for ct in range(NCT):
                    toks.append(qtr.put(ap2(xqT[alt], ct * TTILE, TTILE),
                                        sxq[:, ct * P:(ct + 1) * P],
                                        transpose=True))
                state["xq_war"][alt] = toks[-1]
                st["xqT_toks"][tt] = toks

            def gu(tt):
                alt = tt % 2
                for t_ in st["xqT_toks"][tt]:
                    qtr.wait(TE, t_)
                G.wait(V, st["rounds_v"].get(tt - 1, 0))
                bank_evac = st["bank_evac"]
                buf_free = st["buf_free"]
                tern_war = {}
                for iw in range(NIW):
                    sb_ = iw % 2
                    if iw >= 2:
                        nc.sync.wait_ge(TE.sem, tern_war[iw - 2])
                    gsrc = bass.AP(scr_gT.tensor, iw * IW,
                                   [[II, P], [P * II, NCT], [1, IW]])
                    usrc = bass.AP(scr_uT.tensor, iw * IW,
                                   [[II, P], [P * II, NCT], [1, IW]])
                    tg = qtern.put(gslab[sb_][:], gsrc)
                    tu = qtern.put(uslab[sb_][:], usrc)
                    qtern.wait(TE, tg)
                    qtern.wait(TE, tu)
                    for sub in range(NSUB):
                        bk_g = 2 * sb_
                        bk_u = 2 * sb_ + 1
                        for bk in (bk_g, bk_u):
                            if bk in bank_evac:
                                e_, c_ = bank_evac[bk]
                                TE.wait(e_, c_)
                        for ct in range(NCT):
                            inst = nc.tensor.matmul(psum(bk_g, IW),
                                                    ap2(xqT[alt], ct * TTILE + sub * P, P),
                                                    ap2(gslab[sb_], ct * IW, IW),
                                                    start=(ct == 0),
                                                    stop=(ct == NCT - 1))
                        TE.inc(inst)
                        for ct in range(NCT):
                            inst = nc.tensor.matmul(psum(bk_u, IW),
                                                    ap2(xqT[alt], ct * TTILE + sub * P, P),
                                                    ap2(uslab[sb_], ct * IW, IW),
                                                    start=(ct == 0),
                                                    stop=(ct == NCT - 1))
                        te_mm = TE.inc(inst)
                        S.wait(TE, te_mm)
                        if sb_ in buf_free:
                            S.wait(G, buf_free[sb_])
                        S.inc(nc.scalar.activation(gate_w[sb_][:], psum(bk_g, IW),
                                                   ACT.Silu,
                                                   scale=gsc2[alt][:, sub:sub + 1]))
                        S.inc(nc.scalar.activation(u_w[sb_][:], psum(bk_u, IW),
                                                   ACT.Copy,
                                                   scale=usc2[alt][:, sub:sub + 1]))
                        bank_evac[bk_g] = (S, S.c - 1)
                        bank_evac[bk_u] = (S, S.c)
                        G.wait(S)
                        G.inc(nc.gpsimd.tensor_mul(
                            ap2(hbuf, sub * II + iw * IW, IW),
                            gate_w[sb_][:], u_w[sb_][:]))
                        buf_free[sb_] = G.c
                    tern_war[iw] = TE.c
                state["xqT_te"][alt] = TE.c
                st["gu_te"][tt] = TE.c

            def rounds(tt):
                alt = tt % 2
                shq = scr_hq[alt]
                if state["hq_war"][alt] is not None:
                    qtr.wait(nc.scalar, state["hq_war"][alt])
                V.wait(G)
                if state.get("hqb_tok") is not None:
                    qso.wait(V, state["hqb_tok"])
                V.inc(nc.vector.tensor_reduce(s2v[:, 0:NSUB],
                                              ap3(hbuf, 0, NSUB, II, II), AXX,
                                              ALU.max, apply_absolute_value=True))
                V.wait(V)
                V.inc(nc.vector.tensor_scalar(ln_a[:, 0:NSUB], s2v[:, 0:NSUB],
                                              1.0e-5, None, ALU.max))
                V.wait(V)
                V.inc(nc.vector.tensor_tensor(dsc[:, 0:NSUB], ln_a[:, 0:NSUB],
                                              bass.AP(wsc, 4, [[6, P], [0, NSUB]]),
                                              ALU.mult))
                V.inc(nc.vector.reciprocal(ln_b[:, 0:NSUB], ln_a[:, 0:NSUB]))
                V.wait(V)
                V.inc(nc.vector.tensor_scalar(dsc[:, 0:NSUB], dsc[:, 0:NSUB],
                                              0.5 / 127.0, None, ALU.mult))
                V.inc(nc.vector.tensor_scalar(s2v[:, 0:NSUB], ln_b[:, 0:NSUB],
                                              127.0, None, ALU.mult))
                V.wait(V)
                hq_toks = []
                for sub in range(NSUB):
                    V.wait(V)
                    V.wait(S)
                    V.inc(nc.vector.tensor_scalar(ap2(h05b, 0, II),
                                                  ap2(hbuf, sub * II, II),
                                                  s2v[:, sub:sub + 1], RMAGIC,
                                                  ALU.mult, ALU.add))
                    V.wait(V)
                    if hq_toks:
                        qso.wait(V, hq_toks[-1])
                    V.inc(nc.vector.tensor_scalar(hqi[:], ap2(h05b, 0, II),
                                                  RMAGIC, None, ALU.subtract))
                    V.wait(V)
                    V.inc(nc.vector.tensor_copy(hqb[:], hqi[:]))
                    nc.scalar.wait_ge(V.sem, V.c)
                    hq_toks.append(qso.put(shq[sub * P:(sub + 1) * P, :], hqb[:]))
                state["hqb_tok"] = hq_toks[-1]
                st["rounds_v"][tt] = V.c
                st["hq_toks"][tt] = hq_toks

            def down(tt):
                alt = tt % 2
                shq = scr_hq[alt]
                for t_ in st["hq_toks"][tt]:
                    qso.wait(nc.sync, t_)
                dn_war = {}
                last_hqT = None
                for ic in range(NIT):
                    hb = ic % 2
                    if ic >= 2:
                        nc.sync.wait_ge(TE.sem, dn_war[ic - 2])
                    thq = qtr.put(hqT[hb][:], shq[:, ic * P:(ic + 1) * P],
                                  transpose=True)
                    tds = qtern.put(dslab[hb][:], scr_d[:, ic * P:(ic + 1) * P],
                                    transpose=True)
                    qtr.wait(TE, thq)
                    qtern.wait(TE, tds)
                    if ic == 0:
                        TE.wait(V, st["rounds_v"][tt])
                        if st["dn_evac"]:
                            TE.wait(V, st["dn_evac"])
                        TE.wait(G)
                    for sub in range(NSUB):
                        for cw in range(NCW):
                            inst = nc.tensor.matmul(
                                psum(sub * NCW + cw, CW),
                                ap2(hqT[hb], sub * P, P),
                                ap2(dslab[hb], cw * CW, CW),
                                start=(ic == 0), stop=(ic == NIT - 1))
                    TE.inc(inst)
                    dn_war[ic] = TE.c
                    last_hqT = thq
                state["hq_war"][alt] = last_hqT
                V.wait(TE)
                for sub in range(NSUB):
                    ti = tt * NSUB + sub
                    V.inc(nc.vector.scalar_tensor_tensor(
                        ap2(xres, ti * C, C), psum(sub * NCW, C),
                        dsc[:, sub:sub + 1], ap2(xres, ti * C, C),
                        ALU.mult, ALU.add))
                V.wait(V)
                st["dn_evac"] = V.c
                for bk in range(4):
                    st["bank_evac"][bk] = (V, V.c)

            quant(0)
            for t in range(NTT):
                gu(t)
                if t + 1 < NTT:
                    quant(t + 1)
                rounds(t)
                down(t)
            fctx.close()

        def attn(q):
            phase_n[0] += 1
            pn = phase_n[0]
            actx = contextlib.ExitStack()
            def ab(name, shape, dt=F32):
                return actx.enter_context(nc.sbuf_tensor(f"{name}_{pn}", shape, dt))
            wkT = ab("wkT", [P, NCT, C], BF16)
            wvT = ab("wvT", [P, NCT, C], BF16)
            wrT = ab("wrT", [P, NCT, C], BF16)
            woT = ab("woT", [P, NCT, C], BF16)
            xnT = ab("xnT", [P, NCT, TQP], BF16)
            lrp = ab("lrp", [P, NCT, ACH], BF16)
            rwk = ab("rwk", [P, NCT, ACH], BF16)
            tmpw = ab("tmpw", [P, NCT, LWc])
            oh = ab("oh", [P, NCT, LWc + 1])
            a2w = ab("a2w", [P, NCT, LWc])
            b2w = ab("b2w", [P, NCT, LWc])
            pw_ = ab("pw_", [P, NCT, LWc + 1])
            qw_ = ab("qw_", [P, NCT, LWc + 1])
            dw_ = ab("dw_", [P, NCT, LWc])
            aw_ = a2w
            bw_ = b2w
            nmw = ab("nmw", [P, NCT, LWc])
            dnw = tmpw
            rcw = dw_
            scw = ab("scw", [P, NCT, LWc])
            for t_ in w4_toks:
                qst.wait(nc.sync, t_)
            wt_toks = []
            for wi, wt in enumerate((wkT, wvT, wrT, woT)):
                for ct in range(NCT):
                    wt_toks.append(qtr.put(ap2(wt, ct * C, C),
                                           scr_w4[wi][:, ct * P:(ct + 1) * P],
                                           transpose=True))
            for t_ in wt_toks:
                qtr.wait(TE, t_)
            NXF = TQ // P
            lctx = contextlib.ExitStack()
            xnt_a = lctx.enter_context(
                nc.sbuf_tensor(f"xnt_a_{pn}", [P, max(NSUB, 1), C], F32))
            xqb3a = lctx.enter_context(
                nc.sbuf_tensor(f"xqb3a_{pn}", [P, max(NSUB, 1), C], BF16))
            if state.get("xqb_tok") is not None:
                qso.wait(V, state["xqb_tok"])
                qso.wait(S, state["xqb_tok"])
            xn_toks = []
            xnT_toks_h = []
            NHALF = (NXF + NSUB - 1) // NSUB
            for half in range(NHALF):
                ns = min(NSUB, NXF - half * NSUB)
                ln_tile(xres, half * NSUB * C, ns, xnt_a, 0, xqb3a)
                V.inc(nc.vector.tensor_scalar(
                    ap3(xqb3a, 0, ns, C, C), ap3(xnt_a, 0, ns, C, C),
                    1.0, None, ALU.mult))
                nc.scalar.wait_ge(V.sem, V.c)
                tok = qso.put(
                    bass.AP(scr_xn.tensor, half * NSUB * P * C,
                            [[C, P], [P * C, ns], [1, C]]),
                    ap3(xqb3a, 0, ns, C, C))
                xn_toks.append(tok)
                qso.wait(V, tok)
                qso.wait(nc.scalar, tok)
                qso.wait(nc.sync, tok)
                htoks = []
                for ct in range(NCT):
                    htoks.append(qtr.put(
                        ap2(xnT, ct * TQP + 16 + half * NSUB * P, ns * P),
                        scr_xn[half * NSUB * P:(half * NSUB + ns) * P,
                               ct * P:(ct + 1) * P],
                        transpose=True))
                xnT_toks_h.append(htoks)
            lctx.close()
            state["xqb_tok"] = None
            xkb3 = ab("xkb3", [P, 3, NCT, ACH], BF16)
            k_sl = [ab("k_sb0", [P, NCT, ACH], F16), ab("k_sb1", [P, NCT, ACH], F16)]
            v_sl = [ab("v_sb0", [P, NCT, ACH], F16), ab("v_sb1", [P, NCT, ACH], F16)]
            sr_sl = [ab("sr_sb0", [P, NCT, ACH], F16), ab("sr_sb1", [P, NCT, ACH], F16)]
            half_waited = set()

            def wait_half(h):
                if h not in half_waited:
                    half_waited.add(h)
                    for t_ in xnT_toks_h[h]:
                        qtr.wait(V, t_)
            wait_half(0)
            bdst = ap3(xnT, 15, NCT, TQP, 1)
            V.inc(nc.vector.tensor_copy(bdst, xnbound[:]))
            V.wait(V)

            wkv_vdone = {}
            sig_done = {}
            kvr_te = {}
            wo_te = {}
            lerp_v = {}
            bank_free = {}

            def kvr_lerp(ch):
                t0 = 16 + ch * ACH
                cur = ap3(xnT, t0, NCT, TQP, ACH)
                prv = ap3(xnT, t0 - 1, NCT, TQP, ACH)
                wait_half(min(NHALF - 1, ((ch + 1) * ACH - 1) // (NSUB * P)))
                V.wait(TE, kvr_te.get(ch - 1, 0))
                V.inc(nc.vector.tensor_tensor(lrp[:], cur, prv, ALU.subtract))
                V.wait(V)
                for mi, mix in enumerate((mixk, mixv, mixr)):
                    mix_b = bass.AP(mix, 0, [[NCT, P], [1, NCT], [0, ACH]])
                    xk_mi = ap3(xkb3, mi * NCT * ACH, NCT, ACH, ACH)
                    V.inc(nc.vector.tensor_tensor(xk_mi, lrp[:], mix_b, ALU.mult))
                    V.wait(V)
                    V.inc(nc.vector.tensor_tensor(xk_mi, xk_mi, prv, ALU.add))
                    V.wait(V)
                lerp_v[ch] = V.c

            kvr_pend = []

            def kvr_queue(ch):
                kvr_pend.extend((ch, mi, dch) for mi in range(3)
                                for dch in range(NCT))

            def kvr_pump(n=-1):
                while kvr_pend and n != 0:
                    n -= 1
                    ch, mi, dch = kvr_pend.pop(0)
                    slot = ch % 2
                    wt, dst = ((wkT, k_sl[slot]), (wvT, v_sl[slot]),
                               (wrT, sr_sl[slot]))[mi]
                    if mi == 0 and dch == 0:
                        TE.wait(V, lerp_v[ch])
                        S.wait(V, wkv_vdone.get(ch - 2, 0))
                    bank = 4 + (dch % 2)
                    if bank in bank_free:
                        e_, c_ = bank_free[bank]
                        TE.wait(e_, c_)
                    for ct in range(NCT):
                        inst = nc.tensor.matmul(
                            psum(bank, ACH),
                            ap2(wt, ct * C + dch * P, P),
                            ap2(xkb3, mi * NCT * ACH + ct * ACH, ACH),
                            start=(ct == 0), stop=(ct == NCT - 1))
                    TE.inc(inst)
                    S.wait(TE)
                    S.inc(nc.scalar.activation(
                        ap2(dst, dch * ACH, ACH), psum(bank, ACH),
                        ACT.Sigmoid if mi == 2 else ACT.Copy))
                    bank_free[bank] = (S, S.c)
                    if mi == 2 and dch == NCT - 1:
                        kvr_te[ch] = TE.c
                        sig_done[ch] = S.c

            kvr_lerp(0)
            kvr_queue(0)
            kvr_pump()
            NTSUB = ACH // P
            vdone_sub = {}
            wo_te2 = {}
            pair_guard = {0: 0, 1: 0}
            lws_needed = [min(NLW - 1, ((t + 1) * P - 1) // LWc)
                          for t in range(NTSUB)]

            def wo_mms(ch, tsub):
                pr = tsub % 2
                banks = (0, 1) if pr == 0 else (6, 7)
                TE.wait(V, vdone_sub[(ch, lws_needed[tsub])])
                TE.wait(V, pair_guard[pr])
                for cw in range(NCW):
                    bank = banks[cw % 2]
                    for ct in range(NCT):
                        inst = nc.tensor.matmul(
                            psum(bank, CW),
                            ap2(rwk, ct * ACH + tsub * P, P),
                            ap2(woT, ct * C + cw * CW, CW),
                            start=(ct == 0), stop=(ct == NCT - 1))
                    TE.inc(inst)
                wo_te2[(ch, tsub)] = TE.c

            def wo_evac(ch, tsub):
                pr = tsub % 2
                banks = (0, 1) if pr == 0 else (6, 7)
                ti = (ch * ACH) // P + tsub
                V.wait(TE, wo_te2[(ch, tsub)])
                for cw in range(NCW):
                    bank = banks[cw % 2]
                    V.inc(nc.vector.tensor_add(
                        ap2(xres, ti * C + cw * CW, CW),
                        ap2(xres, ti * C + cw * CW, CW),
                        psum(bank, CW)))
                pair_guard[pr] = V.c

            PUMP_N = (2 * NCT * 3 + 6 * NLW - 1) // (6 * NLW) * 2
            for ch in range(NACH):
                if ch + 1 < NACH:
                    kvr_lerp(ch + 1)
                    kvr_queue(ch + 1)
                if (ch - 1, NTSUB - 1) in wo_te2:
                    wo_evac(ch - 1, NTSUB - 1)
                k_sb = k_sl[ch % 2]
                v_sb = v_sl[ch % 2]
                sr_sb = sr_sl[ch % 2]
                V.wait(S, sig_done[ch])
                next_t = 0
                for lw in range(NLW):
                    ko = lw * LWc
                    kap = ap3(k_sb, ko, NCT, ACH, LWc)
                    vap = ap3(v_sb, ko, NCT, ACH, LWc)
                    srap = ap3(sr_sb, ko, NCT, ACH, LWc)
                    rwap = ap3(rwk, ko, NCT, ACH, LWc)
                    V.wait(V)
                    for t_ in range(ko // P, min(NTSUB - 1,
                                                 (ko + LWc - 1) // P) + 1):
                        V.wait(TE, wo_te2.get((ch - 1, t_), 0))
                    V.inc(nc.vector.tensor_tensor(tmpw[:], kap, ramp[:], ALU.add))
                    V.wait(V)
                    V.inc(nc.vector.tensor_copy(ap3(oh, 0, NCT, LW1, 1), stO[:]))
                    V.wait(V)
                    for ct in range(NCT):
                        V.inc(nc.vector.tensor_tensor_scan(
                            ap2(oh, ct * LW1 + 1, LWc), ap2(tmpw, ct * LWc, LWc),
                            ap2(tmpw, ct * LWc, LWc), stO[:, ct:ct + 1],
                            ALU.max, ALU.max))
                    V.wait(V)
                    oprev = ap3(oh, 0, NCT, LW1, LWc)
                    ocur = ap3(oh, 1, NCT, LW1, LWc)
                    V.inc(nc.vector.tensor_tensor(a2w[:], oprev, ocur, ALU.subtract))
                    V.inc(nc.vector.tensor_tensor(b2w[:], tmpw[:], ocur, ALU.subtract))
                    S.wait(V)
                    S.inc(nc.scalar.activation(a2w[:], a2w[:], ACT.Exp))
                    se1 = S.inc(nc.scalar.activation(b2w[:], b2w[:], ACT.Exp))
                    kvr_pump(PUMP_N)
                    V.wait(S, se1)
                    V.inc(nc.vector.tensor_tensor(scw[:], b2w[:], vap, ALU.mult))
                    V.inc(nc.vector.tensor_copy(ap3(pw_, 0, NCT, LW1, 1), stP[:]))
                    V.inc(nc.vector.tensor_copy(ap3(qw_, 0, NCT, LW1, 1), stQ[:]))
                    V.wait(V)
                    for ct in range(NCT):
                        V.inc(nc.vector.tensor_tensor_scan(
                            ap2(pw_, ct * LW1 + 1, LWc), ap2(a2w, ct * LWc, LWc),
                            ap2(scw, ct * LWc, LWc), stP[:, ct:ct + 1],
                            ALU.mult, ALU.add))
                        V.inc(nc.vector.tensor_tensor_scan(
                            ap2(qw_, ct * LW1 + 1, LWc), ap2(a2w, ct * LWc, LWc),
                            ap2(b2w, ct * LWc, LWc), stQ[:, ct:ct + 1],
                            ALU.mult, ALU.add))
                    V.wait(V)
                    for ct in range(NCT):
                        V.inc(nc.vector.scalar_tensor_tensor(
                            ap2(dw_, ct * LWc, LWc), ap2(oh, ct * LW1, LWc),
                            cstw[:, ct:ct + 1], ap2(tmpw, ct * LWc, LWc),
                            ALU.add, ALU.subtract))
                    V.wait(V)
                    V.inc(nc.vector.tensor_scalar_min(aw_[:], dw_[:], 0.0))
                    V.inc(nc.vector.tensor_scalar(bw_[:], dw_[:], -1.0, 0.0,
                                                  ALU.mult, ALU.min))
                    S.wait(V)
                    S.inc(nc.scalar.activation(aw_[:], aw_[:], ACT.Exp))
                    se2 = S.inc(nc.scalar.activation(bw_[:], bw_[:], ACT.Exp))
                    kvr_pump(PUMP_N)
                    pprev = ap3(pw_, 0, NCT, LW1, LWc)
                    qprev = ap3(qw_, 0, NCT, LW1, LWc)
                    V.wait(S, se2)
                    V.inc(nc.vector.tensor_tensor(dnw[:], aw_[:], qprev, ALU.mult))
                    V.wait(V)
                    V.inc(nc.vector.tensor_add(dnw[:], dnw[:], bw_[:]))
                    S.wait(V)
                    S.inc(nc.scalar.activation(rcw[:], dnw[:], ACT.Ln))
                    S.wait(S)
                    se3 = S.inc(nc.scalar.activation(rcw[:], rcw[:], ACT.Exp,
                                                     scale=-1.0))
                    kvr_pump(PUMP_N)
                    V.inc(nc.vector.tensor_tensor(nmw[:], aw_[:], pprev, ALU.mult))
                    V.wait(V)
                    V.inc(nc.vector.tensor_tensor(scw[:], bw_[:], vap, ALU.mult))
                    V.wait(V)
                    V.inc(nc.vector.tensor_add(nmw[:], nmw[:], scw[:]))
                    V.wait(V)
                    V.wait(S, se3)
                    V.inc(nc.vector.tensor_mul(nmw[:], nmw[:], rcw[:]))
                    V.wait(V)
                    V.inc(nc.vector.tensor_tensor(rwap, nmw[:], srap, ALU.mult))
                    V.inc(nc.vector.tensor_sub(stO[:], ap3(oh, LWc, NCT, LW1, 1),
                                               lwexp[:]))
                    V.inc(nc.vector.tensor_copy(stP[:], ap3(pw_, LWc, NCT, LW1, 1)))
                    V.inc(nc.vector.tensor_copy(stQ[:], ap3(qw_, LWc, NCT, LW1, 1)))
                    V.wait(V)
                    vdone_sub[(ch, lw)] = V.c
                    while next_t < NTSUB and lws_needed[next_t] <= lw:
                        wo_mms(ch, next_t)
                        next_t += 1
                kvr_pump()
                wkv_vdone[ch] = vdone_sub[(ch, NLW - 1)]
                for t_ in range(NTSUB - 1):
                    wo_evac(ch, t_)
            wo_evac(NACH - 1, NTSUB - 1)
            V.wait(V)
            blast = ap3(xnT, 16 + TQ - 1, NCT, TQP, 1)
            V.inc(nc.vector.tensor_copy(xnbound[:], blast))
            V.wait(V)
            actx.close()

        out_toks = []
        for q in range(NQ):
            if q == 0:
                tx = tx0
            else:
                xsrc = bass.AP(x_in.tensor, q * TQ * C,
                               [[C, P], [P * C, TQ // P], [1, C]])
                tx = qx.put(xres[:], xsrc)
            for E_ in (V, S, TE, G):
                qx.wait(E_, tx)
            ffn(q)
            barrier()
            attn(q)
            barrier()
            ffn(q)
            barrier()
            odst = bass.AP(out.tensor, q * TQ * C,
                           [[C, P], [P * C, TQ // P], [1, C]])
            nc.gpsimd.wait_ge(V.sem, V.c)
            ot = qst.put(odst, xres[:])
            out_toks.append(ot)
            qst.wait(nc.sync, ot)
        for t_ in out_toks:
            qst.wait(nc.gpsimd, t_)
            qst.wait(nc.scalar, t_)
    return nc


def kernel(L0_attn_x, L0_attn_w, L0_attn_u, L0_attn_mix_k, L0_attn_mix_v,
           L0_attn_mix_r, L0_attn_Wk, L0_attn_Wv, L0_attn_Wr, L0_attn_Wo,
           L0_ffn_w_gate, L0_ffn_w_up, L0_ffn_w_down):
    x = np.ascontiguousarray(np.asarray(L0_attn_x, dtype=np.float32))
    B, T, C = x.shape
    w = {
        "Wk": np.ascontiguousarray(np.asarray(L0_attn_Wk, np.float32)),
        "Wv": np.ascontiguousarray(np.asarray(L0_attn_Wv, np.float32)),
        "Wr": np.ascontiguousarray(np.asarray(L0_attn_Wr, np.float32)),
        "Wo": np.ascontiguousarray(np.asarray(L0_attn_Wo, np.float32)),
        "Wg": np.ascontiguousarray(np.asarray(L0_ffn_w_gate, np.float32)),
        "Wu": np.ascontiguousarray(np.asarray(L0_ffn_w_up, np.float32)),
        "Wd": np.ascontiguousarray(np.asarray(L0_ffn_w_down, np.float32)),
        "pw": np.ascontiguousarray(np.asarray(L0_attn_w, np.float32)),
        "pu": np.ascontiguousarray(np.asarray(L0_attn_u, np.float32)),
        "pmk": np.ascontiguousarray(np.asarray(L0_attn_mix_k, np.float32)),
        "pmv": np.ascontiguousarray(np.asarray(L0_attn_mix_v, np.float32)),
        "pmr": np.ascontiguousarray(np.asarray(L0_attn_mix_r, np.float32)),
    }
    nc = build_nc(T=T, C=C, II=int(L0_ffn_w_gate.shape[0]))
    in_maps = [dict(w, x=np.ascontiguousarray(x[b])) for b in range(B)]
    res = run_bass_kernel_spmd(nc, in_maps, core_ids=list(range(B)))
    outp = np.stack([np.asarray(res.results[b]["out"], dtype=np.float32)
                     for b in range(B)], axis=0)
    return np.ascontiguousarray(outp)


# version 33
# speedup vs baseline: 1.0205x; 1.0205x over previous
import contextlib
import numpy as np
import concourse.bass as bass
import concourse.mybir as mybir
from concourse.bass_utils import run_bass_kernel_spmd

F32 = mybir.dt.float32
BF16 = mybir.dt.bfloat16
F16 = mybir.dt.float16
I8 = mybir.dt.int8
ALU = mybir.AluOpType
ACT = mybir.ActivationFunctionType
AXX = mybir.AxisListType.X
P = 128
NEG = -1.0e30
RMAGIC = 12582912.0


def _flat(t):
    return int(np.prod(t.shape[1:]))


def ap2(t, off, n, pcnt=P):
    return bass.AP(t, off, [[_flat(t), pcnt], [1, n]])


def ap3(t, off, d1, s1, d2, s2=1, pcnt=P):
    return bass.AP(t, off, [[_flat(t), pcnt], [s1, d1], [s2, d2]])


class Eng:
    def __init__(self, nc, ctx, eng, name):
        self.e = eng
        self.sem = ctx.enter_context(nc.semaphore(name))
        self.c = 0

    def inc(self, inst):
        self.c += 1
        inst.then_inc(self.sem, 1)
        return self.c

    def wait(self, other, cnt=None):
        c = other.c if cnt is None else cnt
        if c > 0:
            self.e.wait_ge(other.sem, c)


class DmaQ:

    def __init__(self, nc, ctx, name, eng, k=3):
        self.eng = eng
        self.sems = [ctx.enter_context(nc.semaphore(f"{name}{i}")) for i in range(k)]
        self.cnt = [0] * k
        self.n = 0

    def put(self, out, in_, transpose=False, slow=False):
        i = self.n % len(self.sems)
        if self.cnt[i]:
            self.eng.wait_ge(self.sems[i], self.cnt[i] * 16)
        inst = (self.eng.dma_start_transpose(out, in_) if transpose
                else self.eng.dma_start(out, in_, allow_slow_non_contiguous=slow))
        inst.then_inc(self.sems[i], 16)
        self.cnt[i] += 1
        self.n += 1
        return (self.sems[i], self.cnt[i] * 16)

    def wait(self, engine, tok):
        e = engine.e if isinstance(engine, Eng) else engine
        e.wait_ge(tok[0], tok[1])

    def wait_all(self, engine):
        e = engine.e if isinstance(engine, Eng) else engine
        for s, c in zip(self.sems, self.cnt):
            if c:
                e.wait_ge(s, c * 16)


def build_nc(T=4096, C=1024, II=4096, TQ=1024, TTILE=512, ACH=256, LWc=128):
    NCT = C // P
    NIT = II // P
    NQ = T // TQ
    NTT = TQ // TTILE
    NSUB = TTILE // P
    IW = min(512, II)
    NIW = II // IW
    CW = min(512, C)
    NCW = C // CW
    NACH = TQ // ACH
    NLW = ACH // LWc
    TQP = TQ + 16

    nc = bass.Bass("TRN2", target_bir_lowering=False)

    x_in = nc.dram_tensor("x", [T, C], F32, kind="ExternalInput").ap()
    out = nc.dram_tensor("out", [T, C], F32, kind="ExternalOutput").ap()
    wk_in = nc.dram_tensor("Wk", [C, C], F32, kind="ExternalInput").ap()
    wv_in = nc.dram_tensor("Wv", [C, C], F32, kind="ExternalInput").ap()
    wr_in = nc.dram_tensor("Wr", [C, C], F32, kind="ExternalInput").ap()
    wo_in = nc.dram_tensor("Wo", [C, C], F32, kind="ExternalInput").ap()
    wg_in = nc.dram_tensor("Wg", [II, C], F32, kind="ExternalInput").ap()
    wu_in = nc.dram_tensor("Wu", [II, C], F32, kind="ExternalInput").ap()
    wd_in = nc.dram_tensor("Wd", [C, II], F32, kind="ExternalInput").ap()
    p_in = [nc.dram_tensor(nm, [C], F32, kind="ExternalInput").ap()
            for nm in ("pw", "pu", "pmk", "pmv", "pmr")]

    scr_g = nc.dram_tensor("scr_g", [II, C], BF16, kind="Internal").ap()
    scr_u = nc.dram_tensor("scr_u", [II, C], BF16, kind="Internal").ap()
    scr_d = nc.dram_tensor("scr_d", [C, II], BF16, kind="Internal").ap()
    scr_gT = nc.dram_tensor("scr_gT", [C, II], BF16, kind="Internal").ap()
    scr_uT = nc.dram_tensor("scr_uT", [C, II], BF16, kind="Internal").ap()
    scr_dT = nc.dram_tensor("scr_dT", [II, C], BF16, kind="Internal").ap()
    scr_w4 = [nc.dram_tensor(f"scr_w4_{i}", [C, C], BF16, kind="Internal").ap()
              for i in range(4)]
    scr_xq = [nc.dram_tensor(f"scr_xq{i}", [TTILE, C], BF16, kind="Internal").ap()
              for i in range(2)]
    scr_hq = [nc.dram_tensor(f"scr_hq{i}", [TTILE, II], BF16, kind="Internal").ap()
              for i in range(2)]
    scr_xn = nc.dram_tensor("scr_xn", [TQ, C], BF16, kind="Internal").ap()

    ctx = contextlib.ExitStack()
    with ctx:
        def sbt(name, shape, dt=F32):
            return ctx.enter_context(nc.sbuf_tensor(name, shape, dt))

        TE = Eng(nc, ctx, nc.tensor, "eT")
        V = Eng(nc, ctx, nc.vector, "eV")
        S = Eng(nc, ctx, nc.scalar, "eS")
        G = Eng(nc, ctx, nc.gpsimd, "eG")
        qx = DmaQ(nc, ctx, "qx", nc.sync, k=5)
        qtern = DmaQ(nc, ctx, "qt", nc.sync, k=4)
        qtr = DmaQ(nc, ctx, "qr", nc.sync, k=4)
        qst = DmaQ(nc, ctx, "qs", nc.gpsimd, k=4)
        qso = DmaQ(nc, ctx, "qo", nc.scalar, k=4)
        ALLQ = [qx, qtern, qtr, qst, qso]

        def barrier():
            for a in (TE, V, S, G):
                for b in (TE, V, S, G):
                    if a is not b:
                        a.wait(b)
                for q_ in ALLQ:
                    q_.wait_all(a)
            for b in (TE, V, S, G):
                if b.c:
                    nc.sync.wait_ge(b.sem, b.c)

        psA = ctx.enter_context(nc.psum_tensor("psA", [P, 8, 512], F32))

        def psum(bank, n, pcnt=P):
            return ap2(psA, bank * 512, n, pcnt)

        xres = sbt("xres", [P, TQ // P, C])
        wexp = sbt("wexp", [P, NCT])
        lwexp = sbt("lwexp", [P, NCT])
        cstw = sbt("cstw", [P, NCT])
        mixk = sbt("mixk", [P, NCT])
        mixv = sbt("mixv", [P, NCT])
        mixr = sbt("mixr", [P, NCT])
        ramp = sbt("ramp", [P, NCT, LWc])
        stO = sbt("stO", [P, NCT])
        stP = sbt("stP", [P, NCT])
        stQ = sbt("stQ", [P, NCT])
        onescol = sbt("onescol", [P, 1])
        onesrow = sbt("onesrow", [1, P])
        wsc = sbt("wsc", [P, 6])
        xnbound = sbt("xnbound", [P, NCT], BF16)

        params_raw = sbt("params_raw", [P, NCT * 5])
        iota = sbt("iota", [P, LWc])
        ptoks = []
        for i, pin in enumerate(p_in):
            src = bass.AP(pin.tensor, 0, [[1, P], [P, NCT]])
            ptoks.append(qx.put(ap2(params_raw, i * NCT, NCT), src, slow=True))
        for t_ in ptoks:
            qx.wait(V, t_)
            qx.wait(S, t_)
        S.inc(nc.scalar.activation(wexp[:], ap2(params_raw, 0, NCT), ACT.Exp))
        V.wait(S)
        V.inc(nc.vector.tensor_sub(cstw[:], wexp[:], ap2(params_raw, NCT, NCT)))
        V.inc(nc.vector.tensor_scalar_mul(lwexp[:], wexp[:], float(LWc)))
        V.inc(nc.vector.tensor_copy(mixk[:], ap2(params_raw, 2 * NCT, NCT)))
        V.inc(nc.vector.tensor_copy(mixv[:], ap2(params_raw, 3 * NCT, NCT)))
        V.inc(nc.vector.tensor_copy(mixr[:], ap2(params_raw, 4 * NCT, NCT)))
        V.inc(nc.vector.memset(onescol[:], 1.0))
        V.inc(nc.vector.memset(onesrow[:], 1.0))
        V.inc(nc.vector.memset(iota[:], 1.0))
        V.inc(nc.vector.memset(stO[:], NEG))
        V.inc(nc.vector.memset(stP[:], 0.0))
        V.inc(nc.vector.memset(stQ[:], 0.0))
        V.inc(nc.vector.memset(xnbound[:], 0.0))
        V.wait(V)
        V.inc(nc.vector.tensor_tensor_scan(iota[:], iota[:], iota[:], 0.0,
                                           ALU.mult, ALU.add))
        V.wait(V)
        iota_b = bass.AP(iota, 0, [[LWc, P], [0, NCT], [1, LWc]])
        wexp_b = bass.AP(wexp, 0, [[NCT, P], [1, NCT], [0, LWc]])
        V.inc(nc.vector.tensor_tensor(ramp[:], iota_b, wexp_b, ALU.mult))

        w4_toks = [qst.put(scr_w4[wi][:, :], win[:, :])
                   for wi, win in enumerate((wk_in, wv_in, wr_in, wo_in))]

        x0src = bass.AP(x_in.tensor, 0, [[C, P], [P * C, TQ // P], [1, C]])
        tx0 = qx.put(xres[:], x0src)

        WMAX = max(C, II)
        BELEM = 4096
        sctx = contextlib.ExitStack()
        def sbt0(name, shape, dt=F32):
            return sctx.enter_context(nc.sbuf_tensor(name, shape, dt))
        wbat = [sbt0(f"wbat{i}", [P, BELEM]) for i in range(4)]
        qbat = sbt0("qbat", [P, BELEM])
        ibat = [sbt0(f"ibat{i}", [P, BELEM], I8) for i in range(2)]
        bbat = [sbt0(f"bbat{i}", [P, BELEM], BF16) for i in range(2)]
        mbat = [sbt0(f"mbat{i}", [P, BELEM]) for i in range(2)]
        obat = sbt0("obat", [P, BELEM // 2])
        acc = sbt0("acc", [P, 1])
        red = sbt0("red", [P, 48])
        rowbuf = sbt0("rowbuf", [1, P])
        sca = sbt0("sca", [1, 2])
        trow = [sbt0(f"trow{i}", [P, WMAX], BF16) for i in range(3)]
        qg = DmaQ(nc, ctx, "qg", nc.sync, k=4)
        qm = DmaQ(nc, ctx, "qm", nc.scalar, k=4)
        ALLQ.extend([qg, qm])
        scr_red = nc.dram_tensor("scr_red", [P], F32, kind="Internal").ap()
        scr_sca = nc.dram_tensor("scr_sca", [2], F32, kind="Internal").ap()

        wlist = [(2, wd_in, scr_d, scr_dT, C, II, True),
                 (0, wg_in, scr_g, scr_gT, II, C, True),
                 (1, wu_in, scr_u, scr_uT, II, C, True)]
        def nbat(RR, CC2):
            BW_ = min(BELEM // CC2, RR // P)
            return RR // P // BW_
        mean_sdone = {}
        mb_guard = [0, 0]
        wb_guard = [0, 0, 0, 0]
        ib_guard = [0, 0]
        bb_guard = [None, None]
        sc_prev = {"t1": None, "t2": None, "t3": None}

        def mean_batch(oi, win, RR, CC2, nb):
            BW_ = min(BELEM // CC2, RR // P)
            BN = BW_ * CC2
            sd = mean_sdone.setdefault(oi, {})
            b = nb % 2
            if mb_guard[b]:
                nc.scalar.wait_ge(S.sem, mb_guard[b])
            tok = qm.put(ap2(mbat[b], 0, BN),
                         win[nb * BW_ * P:(nb + 1) * BW_ * P, :])
            qm.wait(S, tok)
            BH = BN // 2
            for h in range(2):
                S.wait(S)
                S.inc(nc.scalar.activation(
                    ap2(obat, 0, BH), ap2(mbat[b], h * BH, BH), ACT.Abs,
                    accum_out=red[:, 16 * oi + 2 * nb + h:
                                  16 * oi + 2 * nb + h + 1]))
            sd[nb] = S.c
            mb_guard[b] = S.c

        def scale_of(oi, widx, RR, CC2):
            NB = nbat(RR, CC2)
            V.wait(S, mean_sdone[oi][NB - 1])
            if sc_prev["t1"] is not None:
                qso.wait(V, sc_prev["t1"])
            V.inc(nc.vector.tensor_reduce(acc[:],
                                          red[:, 16 * oi:16 * oi + 2 * NB],
                                          AXX, ALU.add))
            V.wait(V)
            nc.scalar.wait_ge(V.sem, V.c)
            if sc_prev["t2"] is not None:
                qm.wait(nc.scalar, sc_prev["t2"])
            t1 = qso.put(bass.AP(scr_red.tensor, 0, [[1, P], [1, 1]]), acc[:])
            sc_prev["t1"] = t1
            qso.wait(nc.scalar, t1)
            t2 = qm.put(rowbuf[0:1, :],
                        bass.AP(scr_red.tensor, 0, [[P, 1], [1, P]]))
            sc_prev["t2"] = t2
            qm.wait(S, t2)
            if sc_prev["t3"] is not None:
                qso.wait(S, sc_prev["t3"])
            S.wait(S)
            S.inc(nc.scalar.activation(ap2(obat, 0, P, 1), rowbuf[0:1, :],
                                       ACT.Copy, accum_out=sca[0:1, 0:1],
                                       scale=1.0 / float(RR * CC2)))
            V.wait(S)
            V.inc(nc.vector.tensor_scalar_max(sca[:1, 0:1], sca[:1, 0:1], 1.0e-5))
            V.wait(V)
            V.inc(nc.vector.reciprocal(sca[:1, 1:2], sca[:1, 0:1]))
            V.wait(V)
            nc.scalar.wait_ge(V.sem, V.c)
            t3 = qso.put(bass.AP(scr_sca.tensor, 0, [[2, 1], [1, 2]]),
                         sca[0:1, 0:2])
            sc_prev["t3"] = t3
            qso.wait(nc.scalar, t3)
            t4 = qm.put(wsc[:, 2 * widx:2 * widx + 2],
                        bass.AP(scr_sca.tensor, 0, [[0, P], [1, 2]]), slow=True)
            qm.wait(V, t4)
            qm.wait(S, t4)

        prev_tr = [None, None, None]
        relayout_pend = []

        def relayout_block(scr_nat, scr_T, RR, cb):
            b = cb % 3
            if prev_tr[b] is not None:
                qst.wait(nc.sync, prev_tr[b])
            tok = qtr.put(ap2(trow[b], 0, RR),
                          scr_nat[:, cb * P:(cb + 1) * P], transpose=True)
            qtr.wait(nc.gpsimd, tok)
            prev_tr[b] = qst.put(scr_T[cb * P:(cb + 1) * P, :],
                                 ap2(trow[b], 0, RR))

        for oi, (widx, win, scr_nat, scr_T, RR, CC2, do_rl) in enumerate(wlist):
            NB = nbat(RR, CC2)
            if oi == 0:
                for nb in range(NB):
                    mean_batch(oi, win, RR, CC2, nb)
            nxt0 = wlist[oi + 1] if oi + 1 < len(wlist) else None
            NBn = nbat(nxt0[4], nxt0[5]) if nxt0 is not None else 0
            scale_of(oi, widx, RR, CC2)
            s_w_ap = wsc[:, 2 * widx + 1:2 * widx + 2]
            BW_ = min(BELEM // CC2, RR // P)
            BN = BW_ * CC2
            vdone = {}
            sdone2 = {}
            st_toks = {}
            pend = relayout_pend
            relayout_pend = []
            rl_per = (len(pend) + NB - 1) // NB if pend else 0
            for nb in range(NB):
                b = nb % 4
                if wb_guard[b]:
                    nc.sync.wait_ge(V.sem, wb_guard[b])
                tok = qg.put(ap2(wbat[b], 0, BN),
                             win[nb * BW_ * P:(nb + 1) * BW_ * P, :])
                qg.wait(V, tok)
                V.inc(nc.vector.tensor_scalar(ap2(qbat, 0, BN),
                                              ap2(wbat[b], 0, BN),
                                              s_w_ap, 1.49, ALU.mult, ALU.min))
                V.wait(V)
                V.inc(nc.vector.tensor_scalar(ap2(wbat[b], 0, BN),
                                              ap2(qbat, 0, BN),
                                              -1.49, RMAGIC, ALU.max, ALU.add))
                V.wait(V)
                b2 = nb % 2
                if ib_guard[b2]:
                    V.wait(S, ib_guard[b2])
                V.inc(nc.vector.tensor_scalar(ap2(ibat[b2], 0, BN),
                                              ap2(wbat[b], 0, BN),
                                              RMAGIC, None, ALU.subtract))
                vdone[nb] = V.c
                wb_guard[b] = V.c
                S.wait(V, vdone[nb])
                if bb_guard[b2] is not None:
                    qso.wait(S, bb_guard[b2])
                S.inc(nc.scalar.activation(ap2(bbat[b2], 0, BN),
                                           ap2(ibat[b2], 0, BN), ACT.Copy))
                sdone2[nb] = S.c
                ib_guard[b2] = S.c
                nc.scalar.wait_ge(S.sem, S.c)
                st_toks[nb] = qso.put(
                    scr_nat[nb * BW_ * P:(nb + 1) * BW_ * P, :],
                    ap2(bbat[b2], 0, BN))
                bb_guard[b2] = st_toks[nb]
                if nxt0 is not None and nb < NBn:
                    mean_batch(oi + 1, nxt0[1], nxt0[4], nxt0[5], nb)
                for _ in range(rl_per):
                    if pend:
                        pend.pop(0)()
            if nxt0 is not None:
                for nb2 in range(NB, NBn):
                    mean_batch(oi + 1, nxt0[1], nxt0[4], nxt0[5], nb2)
            while pend:
                pend.pop(0)()
            if do_rl:
                qso.wait_all(nc.sync)
                relayout_pend = [
                    (lambda sn=scr_nat, st=scr_T, rr=RR, cb_=cb:
                     relayout_block(sn, st, rr, cb_))
                    for cb in range(CC2 // P)]
        while relayout_pend:
            relayout_pend.pop(0)()
        barrier()
        sctx.close()

        lnm = sbt("lnm", [P, 8])
        lnv = sbt("lnv", [P, 8])
        lnr = sbt("lnr", [P, 8])
        lnt = sbt("lnt", [P, 8])
        ln_a = sbt("ln_a", [P, 8])
        ln_b = sbt("ln_b", [P, 8])
        sA = sbt("sA", [P, NSUB])
        dsc = sbt("dsc", [P, NSUB])
        invA = sbt("invA", [P, NSUB])
        gsc = sbt("gsc", [P, NSUB])
        usc = sbt("usc", [P, NSUB])
        s2v = sbt("s2v", [P, NSUB])
        phase_n = [0]

        LW1 = LWc + 1

        def ln_tile(src_t, src_off, ns, dst_t, dst_off, sq_t):
            x3 = ap3(src_t, src_off, ns, C, C)
            d3 = ap3(dst_t, dst_off, ns, C, C)
            S.wait(V)
            V.inc(nc.vector.tensor_reduce(lnm[:, 0:ns], x3, AXX, ALU.add))
            for j in range(ns):
                S.inc(nc.scalar.activation(ap2(sq_t, j * C, C),
                                           ap2(src_t, src_off + j * C, C),
                                           ACT.Square, accum_out=lnv[:, j:j + 1]))
            V.wait(V)
            V.inc(nc.vector.tensor_scalar(lnm[:, 0:ns], lnm[:, 0:ns], 1.0 / C,
                                          None, ALU.mult))
            V.wait(V)
            V.inc(nc.vector.tensor_tensor(lnt[:, 0:ns], lnm[:, 0:ns], lnm[:, 0:ns],
                                          ALU.mult))
            V.wait(S)
            V.inc(nc.vector.tensor_scalar(lnv[:, 0:ns], lnv[:, 0:ns], 1.0 / C,
                                          1.0e-5, ALU.mult, ALU.add))
            V.wait(V)
            V.inc(nc.vector.tensor_sub(lnv[:, 0:ns], lnv[:, 0:ns], lnt[:, 0:ns]))
            V.wait(V)
            V.inc(nc.vector.reciprocal(ln_a[:, 0:ns], lnv[:, 0:ns]))
            S.wait(V)
            S.inc(nc.scalar.activation(lnt[:, 0:ns], ln_a[:, 0:ns], ACT.Sqrt))
            V.wait(S)
            V.inc(nc.vector.tensor_tensor(ln_b[:, 0:ns], lnt[:, 0:ns], lnt[:, 0:ns],
                                          ALU.mult))
            V.wait(V)
            V.inc(nc.vector.tensor_tensor(ln_b[:, 0:ns], lnv[:, 0:ns], ln_b[:, 0:ns],
                                          ALU.mult))
            V.wait(V)
            V.inc(nc.vector.tensor_scalar(ln_a[:, 0:ns], ln_b[:, 0:ns], -0.5, 1.5,
                                          ALU.mult, ALU.add))
            V.wait(V)
            V.inc(nc.vector.tensor_tensor(lnr[:, 0:ns], lnt[:, 0:ns], ln_a[:, 0:ns],
                                          ALU.mult))
            V.wait(V)
            m_bc = bass.AP(lnm, 0, [[8, P], [1, ns], [0, C]])
            r_bc = bass.AP(lnr, 0, [[8, P], [1, ns], [0, C]])
            V.inc(nc.vector.tensor_tensor(d3, x3, m_bc, ALU.subtract))
            V.wait(V)
            V.inc(nc.vector.tensor_tensor(d3, d3, r_bc, ALU.mult))
            V.wait(V)

        state = {"xqT_te": {}, "xq_war": [None, None], "hq_war": [None, None]}

        def ffn(q):
            phase_n[0] += 1
            pn = phase_n[0]
            fctx = contextlib.ExitStack()
            def fb(name, shape, dt=F32):
                return fctx.enter_context(nc.sbuf_tensor(f"{name}_{pn}", shape, dt))
            xqT = [fb(f"xqT{i}", [P, NCT, TTILE], BF16) for i in range(2)]
            gate_w = [fb(f"gate_w{i}", [P, IW]) for i in range(2)]
            u_w = [fb(f"u_w{i}", [P, IW]) for i in range(2)]
            hbuf = fb("hbuf", [P, NSUB, II], F16)
            hqb = fb("hqb", [P, II], BF16)
            xnt_ = fb("xnt_", [P, NSUB, C])
            xqb3 = fb("xqb3", [P, NSUB, C], BF16)
            xqi3 = fb("xqi3", [P, NSUB, C], I8)
            hqi = fb("hqi", [P, II], I8)
            h05b = xnt_ if NSUB * C >= II else fb("h05b", [P, II])
            gslab = [fb(f"gslab{i}", [P, NCT, IW], BF16) for i in range(2)]
            uslab = [fb(f"uslab{i}", [P, NCT, IW], BF16) for i in range(2)]
            dslab = [fb(f"dslab{i}", [P, C], BF16) for i in range(2)]
            hqT = [fb(f"hqT{i}", [P, TTILE], BF16) for i in range(2)]
            gsc2 = [fb(f"gsc2{i}", [P, NSUB]) for i in range(2)]
            usc2 = [fb(f"usc2{i}", [P, NSUB]) for i in range(2)]
            st = {"xqT_toks": {}, "hq_toks": {}, "rounds_v": {}, "gu_te": {},
                  "dn_evac": 0, "bank_evac": {}, "buf_free": {}}

            def quant(tt):
                alt = tt % 2
                sxq = scr_xq[alt]
                if state["xq_war"][alt] is not None:
                    qtr.wait(nc.scalar, state["xq_war"][alt])
                if state.get("xqb_tok") is not None:
                    qso.wait(V, state["xqb_tok"])
                    qso.wait(S, state["xqb_tok"])
                ln_tile(xres, tt * NSUB * C, NSUB, xnt_, 0, xqb3)
                V.inc(nc.vector.tensor_reduce(sA[:, 0:NSUB], xnt_[:], AXX,
                                              ALU.max, apply_absolute_value=True))
                V.wait(V)
                V.inc(nc.vector.tensor_scalar(ln_a[:, 0:NSUB], sA[:, 0:NSUB],
                                              1.0e-5, None, ALU.max))
                V.wait(V)
                V.inc(nc.vector.tensor_scalar(invA[:, 0:NSUB], ln_a[:, 0:NSUB],
                                              1.0 / 127.0, None, ALU.mult))
                V.inc(nc.vector.reciprocal(ln_b[:, 0:NSUB], ln_a[:, 0:NSUB]))
                V.wait(V)
                V.inc(nc.vector.tensor_scalar(sA[:, 0:NSUB], ln_b[:, 0:NSUB],
                                              127.0, None, ALU.mult))
                V.inc(nc.vector.tensor_tensor(gsc2[alt][:], invA[:, 0:NSUB],
                                              bass.AP(wsc, 0, [[6, P], [0, NSUB]]),
                                              ALU.mult))
                V.inc(nc.vector.tensor_tensor(usc2[alt][:], invA[:, 0:NSUB],
                                              bass.AP(wsc, 2, [[6, P], [0, NSUB]]),
                                              ALU.mult))
                V.wait(V)
                sA_bc = bass.AP(sA, 0, [[NSUB, P], [1, NSUB], [0, C]])
                V.inc(nc.vector.tensor_tensor(xnt_[:], xnt_[:], sA_bc, ALU.mult))
                V.wait(V)
                V.inc(nc.vector.tensor_scalar(xqi3[:], xnt_[:], RMAGIC, RMAGIC,
                                              ALU.add, ALU.subtract))
                V.wait(V)
                V.inc(nc.vector.tensor_copy(xqb3[:], xqi3[:]))
                nc.scalar.wait_ge(V.sem, V.c)
                tok = qso.put(
                    bass.AP(sxq.tensor, 0, [[C, P], [P * C, NSUB], [1, C]]),
                    xqb3[:])
                state["xqb_tok"] = tok
                if state["xqT_te"].get(alt):
                    nc.sync.wait_ge(TE.sem, state["xqT_te"][alt])
                qso.wait(nc.sync, tok)
                toks = []
                for ct in range(NCT):
                    toks.append(qtr.put(ap2(xqT[alt], ct * TTILE, TTILE),
                                        sxq[:, ct * P:(ct + 1) * P],
                                        transpose=True))
                state["xq_war"][alt] = toks[-1]
                st["xqT_toks"][tt] = toks

            def gu(tt):
                alt = tt % 2
                for t_ in st["xqT_toks"][tt]:
                    qtr.wait(TE, t_)
                G.wait(V, st["rounds_v"].get(tt - 1, 0))
                bank_evac = st["bank_evac"]
                buf_free = st["buf_free"]
                tern_war = {}
                for iw in range(NIW):
                    sb_ = iw % 2
                    if iw >= 2:
                        nc.sync.wait_ge(TE.sem, tern_war[iw - 2])
                    gsrc = bass.AP(scr_gT.tensor, iw * IW,
                                   [[II, P], [P * II, NCT], [1, IW]])
                    usrc = bass.AP(scr_uT.tensor, iw * IW,
                                   [[II, P], [P * II, NCT], [1, IW]])
                    tg = qtern.put(gslab[sb_][:], gsrc)
                    tu = qtern.put(uslab[sb_][:], usrc)
                    qtern.wait(TE, tg)
                    qtern.wait(TE, tu)
                    for sub in range(NSUB):
                        bk_g = 2 * sb_
                        bk_u = 2 * sb_ + 1
                        for bk in (bk_g, bk_u):
                            if bk in bank_evac:
                                e_, c_ = bank_evac[bk]
                                TE.wait(e_, c_)
                        for ct in range(NCT):
                            inst = nc.tensor.matmul(psum(bk_g, IW),
                                                    ap2(xqT[alt], ct * TTILE + sub * P, P),
                                                    ap2(gslab[sb_], ct * IW, IW),
                                                    start=(ct == 0),
                                                    stop=(ct == NCT - 1))
                        TE.inc(inst)
                        for ct in range(NCT):
                            inst = nc.tensor.matmul(psum(bk_u, IW),
                                                    ap2(xqT[alt], ct * TTILE + sub * P, P),
                                                    ap2(uslab[sb_], ct * IW, IW),
                                                    start=(ct == 0),
                                                    stop=(ct == NCT - 1))
                        te_mm = TE.inc(inst)
                        S.wait(TE, te_mm)
                        if sb_ in buf_free:
                            S.wait(G, buf_free[sb_])
                        S.inc(nc.scalar.activation(gate_w[sb_][:], psum(bk_g, IW),
                                                   ACT.Silu,
                                                   scale=gsc2[alt][:, sub:sub + 1]))
                        S.inc(nc.scalar.activation(u_w[sb_][:], psum(bk_u, IW),
                                                   ACT.Copy,
                                                   scale=usc2[alt][:, sub:sub + 1]))
                        bank_evac[bk_g] = (S, S.c - 1)
                        bank_evac[bk_u] = (S, S.c)
                        G.wait(S)
                        G.inc(nc.gpsimd.tensor_mul(
                            ap2(hbuf, sub * II + iw * IW, IW),
                            gate_w[sb_][:], u_w[sb_][:]))
                        buf_free[sb_] = G.c
                    tern_war[iw] = TE.c
                state["xqT_te"][alt] = TE.c
                st["gu_te"][tt] = TE.c

            def rounds(tt):
                alt = tt % 2
                shq = scr_hq[alt]
                if state["hq_war"][alt] is not None:
                    qtr.wait(nc.scalar, state["hq_war"][alt])
                V.wait(G)
                if state.get("hqb_tok") is not None:
                    qso.wait(V, state["hqb_tok"])
                V.inc(nc.vector.tensor_reduce(s2v[:, 0:NSUB],
                                              ap3(hbuf, 0, NSUB, II, II), AXX,
                                              ALU.max, apply_absolute_value=True))
                V.wait(V)
                V.inc(nc.vector.tensor_scalar(ln_a[:, 0:NSUB], s2v[:, 0:NSUB],
                                              1.0e-5, None, ALU.max))
                V.wait(V)
                V.inc(nc.vector.tensor_tensor(dsc[:, 0:NSUB], ln_a[:, 0:NSUB],
                                              bass.AP(wsc, 4, [[6, P], [0, NSUB]]),
                                              ALU.mult))
                V.inc(nc.vector.reciprocal(ln_b[:, 0:NSUB], ln_a[:, 0:NSUB]))
                V.wait(V)
                V.inc(nc.vector.tensor_scalar(dsc[:, 0:NSUB], dsc[:, 0:NSUB],
                                              0.5 / 127.0, None, ALU.mult))
                V.inc(nc.vector.tensor_scalar(s2v[:, 0:NSUB], ln_b[:, 0:NSUB],
                                              127.0, None, ALU.mult))
                V.wait(V)
                hq_toks = []
                for sub in range(NSUB):
                    V.wait(V)
                    V.wait(S)
                    V.inc(nc.vector.tensor_scalar(ap2(h05b, 0, II),
                                                  ap2(hbuf, sub * II, II),
                                                  s2v[:, sub:sub + 1], RMAGIC,
                                                  ALU.mult, ALU.add))
                    V.wait(V)
                    if hq_toks:
                        qso.wait(V, hq_toks[-1])
                    V.inc(nc.vector.tensor_scalar(hqi[:], ap2(h05b, 0, II),
                                                  RMAGIC, None, ALU.subtract))
                    V.wait(V)
                    V.inc(nc.vector.tensor_copy(hqb[:], hqi[:]))
                    nc.scalar.wait_ge(V.sem, V.c)
                    hq_toks.append(qso.put(shq[sub * P:(sub + 1) * P, :], hqb[:]))
                state["hqb_tok"] = hq_toks[-1]
                st["rounds_v"][tt] = V.c
                st["hq_toks"][tt] = hq_toks

            def down(tt):
                alt = tt % 2
                shq = scr_hq[alt]
                for t_ in st["hq_toks"][tt]:
                    qso.wait(nc.sync, t_)
                dn_war = {}
                last_hqT = None
                for ic in range(NIT):
                    hb = ic % 2
                    if ic >= 2:
                        nc.sync.wait_ge(TE.sem, dn_war[ic - 2])
                    thq = qtr.put(hqT[hb][:], shq[:, ic * P:(ic + 1) * P],
                                  transpose=True)
                    tds = qtern.put(dslab[hb][:], scr_dT[ic * P:(ic + 1) * P, :])
                    qtr.wait(TE, thq)
                    qtern.wait(TE, tds)
                    if ic == 0:
                        TE.wait(V, st["rounds_v"][tt])
                        if st["dn_evac"]:
                            TE.wait(V, st["dn_evac"])
                        TE.wait(G)
                    for sub in range(NSUB):
                        for cw in range(NCW):
                            inst = nc.tensor.matmul(
                                psum(sub * NCW + cw, CW),
                                ap2(hqT[hb], sub * P, P),
                                ap2(dslab[hb], cw * CW, CW),
                                start=(ic == 0), stop=(ic == NIT - 1))
                    TE.inc(inst)
                    dn_war[ic] = TE.c
                    last_hqT = thq
                state["hq_war"][alt] = last_hqT
                V.wait(TE)
                for sub in range(NSUB):
                    ti = tt * NSUB + sub
                    V.inc(nc.vector.scalar_tensor_tensor(
                        ap2(xres, ti * C, C), psum(sub * NCW, C),
                        dsc[:, sub:sub + 1], ap2(xres, ti * C, C),
                        ALU.mult, ALU.add))
                V.wait(V)
                st["dn_evac"] = V.c
                for bk in range(4):
                    st["bank_evac"][bk] = (V, V.c)

            quant(0)
            for t in range(NTT):
                gu(t)
                if t + 1 < NTT:
                    quant(t + 1)
                rounds(t)
                down(t)
            fctx.close()

        def attn(q):
            phase_n[0] += 1
            pn = phase_n[0]
            actx = contextlib.ExitStack()
            def ab(name, shape, dt=F32):
                return actx.enter_context(nc.sbuf_tensor(f"{name}_{pn}", shape, dt))
            wkT = ab("wkT", [P, NCT, C], BF16)
            wvT = ab("wvT", [P, NCT, C], BF16)
            wrT = ab("wrT", [P, NCT, C], BF16)
            woT = ab("woT", [P, NCT, C], BF16)
            xnT = ab("xnT", [P, NCT, TQP], BF16)
            lrp = ab("lrp", [P, NCT, ACH], BF16)
            rwk = ab("rwk", [P, NCT, ACH], BF16)
            tmpw = ab("tmpw", [P, NCT, LWc])
            oh = ab("oh", [P, NCT, LWc + 1])
            a2w = ab("a2w", [P, NCT, LWc])
            b2w = ab("b2w", [P, NCT, LWc])
            pw_ = ab("pw_", [P, NCT, LWc + 1])
            qw_ = ab("qw_", [P, NCT, LWc + 1])
            dw_ = ab("dw_", [P, NCT, LWc])
            aw_ = a2w
            bw_ = b2w
            nmw = ab("nmw", [P, NCT, LWc])
            dnw = tmpw
            rcw = dw_
            scw = ab("scw", [P, NCT, LWc])
            for t_ in w4_toks:
                qst.wait(nc.sync, t_)
            wt_toks = []
            for wi, wt in enumerate((wkT, wvT, wrT, woT)):
                for ct in range(NCT):
                    wt_toks.append(qtr.put(ap2(wt, ct * C, C),
                                           scr_w4[wi][:, ct * P:(ct + 1) * P],
                                           transpose=True))
            for t_ in wt_toks:
                qtr.wait(TE, t_)
            NXF = TQ // P
            lctx = contextlib.ExitStack()
            xnt_a = lctx.enter_context(
                nc.sbuf_tensor(f"xnt_a_{pn}", [P, max(NSUB, 1), C], F32))
            xqb3a = lctx.enter_context(
                nc.sbuf_tensor(f"xqb3a_{pn}", [P, max(NSUB, 1), C], BF16))
            if state.get("xqb_tok") is not None:
                qso.wait(V, state["xqb_tok"])
                qso.wait(S, state["xqb_tok"])
            xn_toks = []
            xnT_toks_h = []
            NHALF = (NXF + NSUB - 1) // NSUB
            for half in range(NHALF):
                ns = min(NSUB, NXF - half * NSUB)
                ln_tile(xres, half * NSUB * C, ns, xnt_a, 0, xqb3a)
                V.inc(nc.vector.tensor_scalar(
                    ap3(xqb3a, 0, ns, C, C), ap3(xnt_a, 0, ns, C, C),
                    1.0, None, ALU.mult))
                nc.scalar.wait_ge(V.sem, V.c)
                tok = qso.put(
                    bass.AP(scr_xn.tensor, half * NSUB * P * C,
                            [[C, P], [P * C, ns], [1, C]]),
                    ap3(xqb3a, 0, ns, C, C))
                xn_toks.append(tok)
                qso.wait(V, tok)
                qso.wait(nc.scalar, tok)
                qso.wait(nc.sync, tok)
                htoks = []
                for ct in range(NCT):
                    htoks.append(qtr.put(
                        ap2(xnT, ct * TQP + 16 + half * NSUB * P, ns * P),
                        scr_xn[half * NSUB * P:(half * NSUB + ns) * P,
                               ct * P:(ct + 1) * P],
                        transpose=True))
                xnT_toks_h.append(htoks)
            lctx.close()
            state["xqb_tok"] = None
            xkb3 = ab("xkb3", [P, 3, NCT, ACH], BF16)
            k_sl = [ab("k_sb0", [P, NCT, ACH], F16), ab("k_sb1", [P, NCT, ACH], F16)]
            v_sl = [ab("v_sb0", [P, NCT, ACH], F16), ab("v_sb1", [P, NCT, ACH], F16)]
            sr_sl = [ab("sr_sb0", [P, NCT, ACH], F16), ab("sr_sb1", [P, NCT, ACH], F16)]
            half_waited = set()

            def wait_half(h):
                if h not in half_waited:
                    half_waited.add(h)
                    for t_ in xnT_toks_h[h]:
                        qtr.wait(V, t_)
            wait_half(0)
            bdst = ap3(xnT, 15, NCT, TQP, 1)
            V.inc(nc.vector.tensor_copy(bdst, xnbound[:]))
            V.wait(V)

            wkv_vdone = {}
            sig_done = {}
            kvr_te = {}
            wo_te = {}
            lerp_v = {}
            bank_free = {}

            def kvr_lerp(ch):
                t0 = 16 + ch * ACH
                cur = ap3(xnT, t0, NCT, TQP, ACH)
                prv = ap3(xnT, t0 - 1, NCT, TQP, ACH)
                wait_half(min(NHALF - 1, ((ch + 1) * ACH - 1) // (NSUB * P)))
                V.wait(TE, kvr_te.get(ch - 1, 0))
                V.inc(nc.vector.tensor_tensor(lrp[:], cur, prv, ALU.subtract))
                V.wait(V)
                for mi, mix in enumerate((mixk, mixv, mixr)):
                    mix_b = bass.AP(mix, 0, [[NCT, P], [1, NCT], [0, ACH]])
                    xk_mi = ap3(xkb3, mi * NCT * ACH, NCT, ACH, ACH)
                    V.inc(nc.vector.tensor_tensor(xk_mi, lrp[:], mix_b, ALU.mult))
                    V.wait(V)
                    V.inc(nc.vector.tensor_tensor(xk_mi, xk_mi, prv, ALU.add))
                    V.wait(V)
                lerp_v[ch] = V.c

            kvr_pend = []

            def kvr_queue(ch):
                kvr_pend.extend((ch, mi, dch) for mi in range(3)
                                for dch in range(NCT))

            def kvr_pump(n=-1):
                while kvr_pend and n != 0:
                    n -= 1
                    ch, mi, dch = kvr_pend.pop(0)
                    slot = ch % 2
                    wt, dst = ((wkT, k_sl[slot]), (wvT, v_sl[slot]),
                               (wrT, sr_sl[slot]))[mi]
                    if mi == 0 and dch == 0:
                        TE.wait(V, lerp_v[ch])
                        S.wait(V, wkv_vdone.get(ch - 2, 0))
                    bank = 4 + (dch % 2)
                    if bank in bank_free:
                        e_, c_ = bank_free[bank]
                        TE.wait(e_, c_)
                    for ct in range(NCT):
                        inst = nc.tensor.matmul(
                            psum(bank, ACH),
                            ap2(wt, ct * C + dch * P, P),
                            ap2(xkb3, mi * NCT * ACH + ct * ACH, ACH),
                            start=(ct == 0), stop=(ct == NCT - 1))
                    TE.inc(inst)
                    S.wait(TE)
                    S.inc(nc.scalar.activation(
                        ap2(dst, dch * ACH, ACH), psum(bank, ACH),
                        ACT.Sigmoid if mi == 2 else ACT.Copy))
                    bank_free[bank] = (S, S.c)
                    if mi == 2 and dch == NCT - 1:
                        kvr_te[ch] = TE.c
                        sig_done[ch] = S.c

            kvr_lerp(0)
            kvr_queue(0)
            kvr_pump()
            NTSUB = ACH // P
            vdone_sub = {}
            wo_te2 = {}
            pair_guard = {0: 0, 1: 0}
            lws_needed = [min(NLW - 1, ((t + 1) * P - 1) // LWc)
                          for t in range(NTSUB)]

            def wo_mms(ch, tsub):
                pr = tsub % 2
                banks = (0, 1) if pr == 0 else (6, 7)
                TE.wait(V, vdone_sub[(ch, lws_needed[tsub])])
                TE.wait(V, pair_guard[pr])
                for cw in range(NCW):
                    bank = banks[cw % 2]
                    for ct in range(NCT):
                        inst = nc.tensor.matmul(
                            psum(bank, CW),
                            ap2(rwk, ct * ACH + tsub * P, P),
                            ap2(woT, ct * C + cw * CW, CW),
                            start=(ct == 0), stop=(ct == NCT - 1))
                    TE.inc(inst)
                wo_te2[(ch, tsub)] = TE.c

            def wo_evac(ch, tsub):
                pr = tsub % 2
                banks = (0, 1) if pr == 0 else (6, 7)
                ti = (ch * ACH) // P + tsub
                V.wait(TE, wo_te2[(ch, tsub)])
                for cw in range(NCW):
                    bank = banks[cw % 2]
                    V.inc(nc.vector.tensor_add(
                        ap2(xres, ti * C + cw * CW, CW),
                        ap2(xres, ti * C + cw * CW, CW),
                        psum(bank, CW)))
                pair_guard[pr] = V.c

            PUMP_N = (2 * NCT * 3 + 6 * NLW - 1) // (6 * NLW) * 2
            for ch in range(NACH):
                if ch + 1 < NACH:
                    kvr_lerp(ch + 1)
                    kvr_queue(ch + 1)
                if (ch - 1, NTSUB - 1) in wo_te2:
                    wo_evac(ch - 1, NTSUB - 1)
                k_sb = k_sl[ch % 2]
                v_sb = v_sl[ch % 2]
                sr_sb = sr_sl[ch % 2]
                V.wait(S, sig_done[ch])
                next_t = 0
                for lw in range(NLW):
                    ko = lw * LWc
                    kap = ap3(k_sb, ko, NCT, ACH, LWc)
                    vap = ap3(v_sb, ko, NCT, ACH, LWc)
                    srap = ap3(sr_sb, ko, NCT, ACH, LWc)
                    rwap = ap3(rwk, ko, NCT, ACH, LWc)
                    V.wait(V)
                    for t_ in range(ko // P, min(NTSUB - 1,
                                                 (ko + LWc - 1) // P) + 1):
                        V.wait(TE, wo_te2.get((ch - 1, t_), 0))
                    V.inc(nc.vector.tensor_tensor(tmpw[:], kap, ramp[:], ALU.add))
                    V.wait(V)
                    V.inc(nc.vector.tensor_copy(ap3(oh, 0, NCT, LW1, 1), stO[:]))
                    V.wait(V)
                    for ct in range(NCT):
                        V.inc(nc.vector.tensor_tensor_scan(
                            ap2(oh, ct * LW1 + 1, LWc), ap2(tmpw, ct * LWc, LWc),
                            ap2(tmpw, ct * LWc, LWc), stO[:, ct:ct + 1],
                            ALU.max, ALU.max))
                    V.wait(V)
                    oprev = ap3(oh, 0, NCT, LW1, LWc)
                    ocur = ap3(oh, 1, NCT, LW1, LWc)
                    V.inc(nc.vector.tensor_tensor(a2w[:], oprev, ocur, ALU.subtract))
                    V.inc(nc.vector.tensor_tensor(b2w[:], tmpw[:], ocur, ALU.subtract))
                    S.wait(V)
                    S.inc(nc.scalar.activation(a2w[:], a2w[:], ACT.Exp))
                    se1 = S.inc(nc.scalar.activation(b2w[:], b2w[:], ACT.Exp))
                    kvr_pump(PUMP_N)
                    V.wait(S, se1)
                    V.inc(nc.vector.tensor_tensor(scw[:], b2w[:], vap, ALU.mult))
                    V.inc(nc.vector.tensor_copy(ap3(pw_, 0, NCT, LW1, 1), stP[:]))
                    V.inc(nc.vector.tensor_copy(ap3(qw_, 0, NCT, LW1, 1), stQ[:]))
                    V.wait(V)
                    for ct in range(NCT):
                        V.inc(nc.vector.tensor_tensor_scan(
                            ap2(pw_, ct * LW1 + 1, LWc), ap2(a2w, ct * LWc, LWc),
                            ap2(scw, ct * LWc, LWc), stP[:, ct:ct + 1],
                            ALU.mult, ALU.add))
                        V.inc(nc.vector.tensor_tensor_scan(
                            ap2(qw_, ct * LW1 + 1, LWc), ap2(a2w, ct * LWc, LWc),
                            ap2(b2w, ct * LWc, LWc), stQ[:, ct:ct + 1],
                            ALU.mult, ALU.add))
                    V.wait(V)
                    for ct in range(NCT):
                        V.inc(nc.vector.scalar_tensor_tensor(
                            ap2(dw_, ct * LWc, LWc), ap2(oh, ct * LW1, LWc),
                            cstw[:, ct:ct + 1], ap2(tmpw, ct * LWc, LWc),
                            ALU.add, ALU.subtract))
                    V.wait(V)
                    V.inc(nc.vector.tensor_scalar_min(aw_[:], dw_[:], 0.0))
                    V.inc(nc.vector.tensor_scalar(bw_[:], dw_[:], -1.0, 0.0,
                                                  ALU.mult, ALU.min))
                    S.wait(V)
                    S.inc(nc.scalar.activation(aw_[:], aw_[:], ACT.Exp))
                    se2 = S.inc(nc.scalar.activation(bw_[:], bw_[:], ACT.Exp))
                    kvr_pump(PUMP_N)
                    pprev = ap3(pw_, 0, NCT, LW1, LWc)
                    qprev = ap3(qw_, 0, NCT, LW1, LWc)
                    V.wait(S, se2)
                    V.inc(nc.vector.tensor_tensor(dnw[:], aw_[:], qprev, ALU.mult))
                    V.wait(V)
                    V.inc(nc.vector.tensor_add(dnw[:], dnw[:], bw_[:]))
                    S.wait(V)
                    S.inc(nc.scalar.activation(rcw[:], dnw[:], ACT.Ln))
                    S.wait(S)
                    se3 = S.inc(nc.scalar.activation(rcw[:], rcw[:], ACT.Exp,
                                                     scale=-1.0))
                    kvr_pump(PUMP_N)
                    V.inc(nc.vector.tensor_tensor(nmw[:], aw_[:], pprev, ALU.mult))
                    V.wait(V)
                    V.inc(nc.vector.tensor_tensor(scw[:], bw_[:], vap, ALU.mult))
                    V.wait(V)
                    V.inc(nc.vector.tensor_add(nmw[:], nmw[:], scw[:]))
                    V.wait(V)
                    V.wait(S, se3)
                    V.inc(nc.vector.tensor_mul(nmw[:], nmw[:], rcw[:]))
                    V.wait(V)
                    V.inc(nc.vector.tensor_tensor(rwap, nmw[:], srap, ALU.mult))
                    V.inc(nc.vector.tensor_sub(stO[:], ap3(oh, LWc, NCT, LW1, 1),
                                               lwexp[:]))
                    V.inc(nc.vector.tensor_copy(stP[:], ap3(pw_, LWc, NCT, LW1, 1)))
                    V.inc(nc.vector.tensor_copy(stQ[:], ap3(qw_, LWc, NCT, LW1, 1)))
                    V.wait(V)
                    vdone_sub[(ch, lw)] = V.c
                    while next_t < NTSUB and lws_needed[next_t] <= lw:
                        wo_mms(ch, next_t)
                        next_t += 1
                kvr_pump()
                wkv_vdone[ch] = vdone_sub[(ch, NLW - 1)]
                for t_ in range(NTSUB - 1):
                    wo_evac(ch, t_)
            wo_evac(NACH - 1, NTSUB - 1)
            V.wait(V)
            blast = ap3(xnT, 16 + TQ - 1, NCT, TQP, 1)
            V.inc(nc.vector.tensor_copy(xnbound[:], blast))
            V.wait(V)
            actx.close()

        out_toks = []
        for q in range(NQ):
            if q == 0:
                tx = tx0
            else:
                xsrc = bass.AP(x_in.tensor, q * TQ * C,
                               [[C, P], [P * C, TQ // P], [1, C]])
                tx = qx.put(xres[:], xsrc)
            for E_ in (V, S, TE, G):
                qx.wait(E_, tx)
            ffn(q)
            barrier()
            attn(q)
            barrier()
            ffn(q)
            barrier()
            odst = bass.AP(out.tensor, q * TQ * C,
                           [[C, P], [P * C, TQ // P], [1, C]])
            nc.gpsimd.wait_ge(V.sem, V.c)
            ot = qst.put(odst, xres[:])
            out_toks.append(ot)
            qst.wait(nc.sync, ot)
        for t_ in out_toks:
            qst.wait(nc.gpsimd, t_)
            qst.wait(nc.scalar, t_)
    return nc


def kernel(L0_attn_x, L0_attn_w, L0_attn_u, L0_attn_mix_k, L0_attn_mix_v,
           L0_attn_mix_r, L0_attn_Wk, L0_attn_Wv, L0_attn_Wr, L0_attn_Wo,
           L0_ffn_w_gate, L0_ffn_w_up, L0_ffn_w_down):
    x = np.ascontiguousarray(np.asarray(L0_attn_x, dtype=np.float32))
    B, T, C = x.shape
    w = {
        "Wk": np.ascontiguousarray(np.asarray(L0_attn_Wk, np.float32)),
        "Wv": np.ascontiguousarray(np.asarray(L0_attn_Wv, np.float32)),
        "Wr": np.ascontiguousarray(np.asarray(L0_attn_Wr, np.float32)),
        "Wo": np.ascontiguousarray(np.asarray(L0_attn_Wo, np.float32)),
        "Wg": np.ascontiguousarray(np.asarray(L0_ffn_w_gate, np.float32)),
        "Wu": np.ascontiguousarray(np.asarray(L0_ffn_w_up, np.float32)),
        "Wd": np.ascontiguousarray(np.asarray(L0_ffn_w_down, np.float32)),
        "pw": np.ascontiguousarray(np.asarray(L0_attn_w, np.float32)),
        "pu": np.ascontiguousarray(np.asarray(L0_attn_u, np.float32)),
        "pmk": np.ascontiguousarray(np.asarray(L0_attn_mix_k, np.float32)),
        "pmv": np.ascontiguousarray(np.asarray(L0_attn_mix_v, np.float32)),
        "pmr": np.ascontiguousarray(np.asarray(L0_attn_mix_r, np.float32)),
    }
    nc = build_nc(T=T, C=C, II=int(L0_ffn_w_gate.shape[0]))
    in_maps = [dict(w, x=np.ascontiguousarray(x[b])) for b in range(B)]
    res = run_bass_kernel_spmd(nc, in_maps, core_ids=list(range(B)))
    outp = np.stack([np.asarray(res.results[b]["out"], dtype=np.float32)
                     for b in range(B)], axis=0)
    return np.ascontiguousarray(outp)


# version 34
# speedup vs baseline: 1.0504x; 1.0293x over previous
import contextlib
import numpy as np
import concourse.bass as bass
import concourse.mybir as mybir
from concourse.bass_utils import run_bass_kernel_spmd

F32 = mybir.dt.float32
BF16 = mybir.dt.bfloat16
F16 = mybir.dt.float16
I8 = mybir.dt.int8
ALU = mybir.AluOpType
ACT = mybir.ActivationFunctionType
AXX = mybir.AxisListType.X
P = 128
NEG = -1.0e30
RMAGIC = 12582912.0


def _flat(t):
    return int(np.prod(t.shape[1:]))


def ap2(t, off, n, pcnt=P):
    return bass.AP(t, off, [[_flat(t), pcnt], [1, n]])


def ap3(t, off, d1, s1, d2, s2=1, pcnt=P):
    return bass.AP(t, off, [[_flat(t), pcnt], [s1, d1], [s2, d2]])


class Eng:
    def __init__(self, nc, ctx, eng, name):
        self.e = eng
        self.sem = ctx.enter_context(nc.semaphore(name))
        self.c = 0

    def inc(self, inst):
        self.c += 1
        inst.then_inc(self.sem, 1)
        return self.c

    def wait(self, other, cnt=None):
        c = other.c if cnt is None else cnt
        if c > 0:
            self.e.wait_ge(other.sem, c)


class DmaQ:

    def __init__(self, nc, ctx, name, eng, k=3):
        self.eng = eng
        self.sems = [ctx.enter_context(nc.semaphore(f"{name}{i}")) for i in range(k)]
        self.cnt = [0] * k
        self.n = 0

    def put(self, out, in_, transpose=False, slow=False):
        i = self.n % len(self.sems)
        if self.cnt[i]:
            self.eng.wait_ge(self.sems[i], self.cnt[i] * 16)
        inst = (self.eng.dma_start_transpose(out, in_) if transpose
                else self.eng.dma_start(out, in_, allow_slow_non_contiguous=slow))
        inst.then_inc(self.sems[i], 16)
        self.cnt[i] += 1
        self.n += 1
        return (self.sems[i], self.cnt[i] * 16)

    def wait(self, engine, tok):
        e = engine.e if isinstance(engine, Eng) else engine
        e.wait_ge(tok[0], tok[1])

    def wait_all(self, engine):
        e = engine.e if isinstance(engine, Eng) else engine
        for s, c in zip(self.sems, self.cnt):
            if c:
                e.wait_ge(s, c * 16)


def build_nc(T=4096, C=1024, II=4096, TQ=1024, TTILE=512, ACH=256, LWc=128):
    NCT = C // P
    NIT = II // P
    NQ = T // TQ
    NTT = TQ // TTILE
    NSUB = TTILE // P
    IW = min(512, II)
    NIW = II // IW
    CW = min(512, C)
    NCW = C // CW
    NACH = TQ // ACH
    NLW = ACH // LWc
    TQP = TQ + 16

    nc = bass.Bass("TRN2", target_bir_lowering=False)

    x_in = nc.dram_tensor("x", [T, C], F32, kind="ExternalInput").ap()
    out = nc.dram_tensor("out", [T, C], F32, kind="ExternalOutput").ap()
    wk_in = nc.dram_tensor("Wk", [C, C], F32, kind="ExternalInput").ap()
    wv_in = nc.dram_tensor("Wv", [C, C], F32, kind="ExternalInput").ap()
    wr_in = nc.dram_tensor("Wr", [C, C], F32, kind="ExternalInput").ap()
    wo_in = nc.dram_tensor("Wo", [C, C], F32, kind="ExternalInput").ap()
    wg_in = nc.dram_tensor("Wg", [II, C], F32, kind="ExternalInput").ap()
    wu_in = nc.dram_tensor("Wu", [II, C], F32, kind="ExternalInput").ap()
    wd_in = nc.dram_tensor("Wd", [C, II], F32, kind="ExternalInput").ap()
    p_in = [nc.dram_tensor(nm, [C], F32, kind="ExternalInput").ap()
            for nm in ("pw", "pu", "pmk", "pmv", "pmr")]

    scr_g = nc.dram_tensor("scr_g", [II, C], BF16, kind="Internal").ap()
    scr_u = nc.dram_tensor("scr_u", [II, C], BF16, kind="Internal").ap()
    scr_d = nc.dram_tensor("scr_d", [C, II], BF16, kind="Internal").ap()
    scr_gT = nc.dram_tensor("scr_gT", [C, II], BF16, kind="Internal").ap()
    scr_uT = nc.dram_tensor("scr_uT", [C, II], BF16, kind="Internal").ap()
    scr_dT = nc.dram_tensor("scr_dT", [II, C], BF16, kind="Internal").ap()
    scr_w4 = [nc.dram_tensor(f"scr_w4_{i}", [C, C], BF16, kind="Internal").ap()
              for i in range(4)]
    scr_xq = [nc.dram_tensor(f"scr_xq{i}", [TTILE, C], BF16, kind="Internal").ap()
              for i in range(2)]
    scr_hq = [nc.dram_tensor(f"scr_hq{i}", [TTILE, II], BF16, kind="Internal").ap()
              for i in range(2)]
    scr_xn = nc.dram_tensor("scr_xn", [TQ, C], BF16, kind="Internal").ap()

    ctx = contextlib.ExitStack()
    with ctx:
        def sbt(name, shape, dt=F32):
            return ctx.enter_context(nc.sbuf_tensor(name, shape, dt))

        TE = Eng(nc, ctx, nc.tensor, "eT")
        V = Eng(nc, ctx, nc.vector, "eV")
        S = Eng(nc, ctx, nc.scalar, "eS")
        G = Eng(nc, ctx, nc.gpsimd, "eG")
        qx = DmaQ(nc, ctx, "qx", nc.sync, k=5)
        qtern = DmaQ(nc, ctx, "qt", nc.sync, k=4)
        qtr = DmaQ(nc, ctx, "qr", nc.sync, k=4)
        qst = DmaQ(nc, ctx, "qs", nc.gpsimd, k=4)
        qso = DmaQ(nc, ctx, "qo", nc.scalar, k=4)
        ALLQ = [qx, qtern, qtr, qst, qso]

        def barrier():
            for a in (TE, V, S, G):
                for b in (TE, V, S, G):
                    if a is not b:
                        a.wait(b)
                for q_ in ALLQ:
                    q_.wait_all(a)
            for b in (TE, V, S, G):
                if b.c:
                    nc.sync.wait_ge(b.sem, b.c)

        psA = ctx.enter_context(nc.psum_tensor("psA", [P, 8, 512], F32))

        def psum(bank, n, pcnt=P):
            return ap2(psA, bank * 512, n, pcnt)

        xres = sbt("xres", [P, TQ // P, C])
        wexp = sbt("wexp", [P, NCT])
        lwexp = sbt("lwexp", [P, NCT])
        cstw = sbt("cstw", [P, NCT])
        mixk = sbt("mixk", [P, NCT])
        mixv = sbt("mixv", [P, NCT])
        mixr = sbt("mixr", [P, NCT])
        ramp = sbt("ramp", [P, NCT, LWc])
        stO = sbt("stO", [P, NCT])
        stP = sbt("stP", [P, NCT])
        stQ = sbt("stQ", [P, NCT])
        onescol = sbt("onescol", [P, 1])
        onesrow = sbt("onesrow", [1, P])
        wsc = sbt("wsc", [P, 6])
        xnbound = sbt("xnbound", [P, NCT], BF16)

        params_raw = sbt("params_raw", [P, NCT * 5])
        iota = sbt("iota", [P, LWc])
        ptoks = []
        for i, pin in enumerate(p_in):
            src = bass.AP(pin.tensor, 0, [[1, P], [P, NCT]])
            ptoks.append(qx.put(ap2(params_raw, i * NCT, NCT), src, slow=True))
        for t_ in ptoks:
            qx.wait(V, t_)
            qx.wait(S, t_)
        S.inc(nc.scalar.activation(wexp[:], ap2(params_raw, 0, NCT), ACT.Exp))
        V.wait(S)
        V.inc(nc.vector.tensor_sub(cstw[:], wexp[:], ap2(params_raw, NCT, NCT)))
        V.inc(nc.vector.tensor_scalar_mul(lwexp[:], wexp[:], float(LWc)))
        V.inc(nc.vector.tensor_copy(mixk[:], ap2(params_raw, 2 * NCT, NCT)))
        V.inc(nc.vector.tensor_copy(mixv[:], ap2(params_raw, 3 * NCT, NCT)))
        V.inc(nc.vector.tensor_copy(mixr[:], ap2(params_raw, 4 * NCT, NCT)))
        V.inc(nc.vector.memset(onescol[:], 1.0))
        V.inc(nc.vector.memset(onesrow[:], 1.0))
        V.inc(nc.vector.memset(iota[:], 1.0))
        V.inc(nc.vector.memset(stO[:], NEG))
        V.inc(nc.vector.memset(stP[:], 0.0))
        V.inc(nc.vector.memset(stQ[:], 0.0))
        V.inc(nc.vector.memset(xnbound[:], 0.0))
        V.wait(V)
        V.inc(nc.vector.tensor_tensor_scan(iota[:], iota[:], iota[:], 0.0,
                                           ALU.mult, ALU.add))
        V.wait(V)
        iota_b = bass.AP(iota, 0, [[LWc, P], [0, NCT], [1, LWc]])
        wexp_b = bass.AP(wexp, 0, [[NCT, P], [1, NCT], [0, LWc]])
        V.inc(nc.vector.tensor_tensor(ramp[:], iota_b, wexp_b, ALU.mult))

        w4_toks = [qst.put(scr_w4[wi][:, :], win[:, :])
                   for wi, win in enumerate((wk_in, wv_in, wr_in, wo_in))]

        x0src = bass.AP(x_in.tensor, 0, [[C, P], [P * C, TQ // P], [1, C]])
        tx0 = qx.put(xres[:], x0src)

        WMAX = max(C, II)
        BELEM = 4096
        sctx = contextlib.ExitStack()
        def sbt0(name, shape, dt=F32):
            return sctx.enter_context(nc.sbuf_tensor(name, shape, dt))
        wbat = [sbt0(f"wbat{i}", [P, BELEM]) for i in range(4)]
        qbat = sbt0("qbat", [P, BELEM])
        ibat = [sbt0(f"ibat{i}", [P, BELEM], I8) for i in range(2)]
        bbat = [sbt0(f"bbat{i}", [P, BELEM], BF16) for i in range(2)]
        mbat = [sbt0(f"mbat{i}", [P, BELEM]) for i in range(2)]
        obat = sbt0("obat", [P, BELEM // 2])
        acc = sbt0("acc", [P, 1])
        red = sbt0("red", [P, 48])
        sca = sbt0("sca", [1, 2])
        trow = [sbt0(f"trow{i}", [P, WMAX], BF16) for i in range(3)]
        qg = DmaQ(nc, ctx, "qg", nc.sync, k=4)
        ALLQ.append(qg)

        wlist = [(2, wd_in, scr_d, scr_dT, C, II),
                 (0, wg_in, scr_g, scr_gT, II, C),
                 (1, wu_in, scr_u, scr_uT, II, C)]
        def nbat(RR, CC2):
            BW_ = min(BELEM // CC2, RR // P)
            return RR // P // BW_
        mean_sdone = {}
        scale_guard = [0]
        mb_guard = [0, 0]
        wb_guard = [0, 0, 0, 0]
        ib_guard = [0, 0]
        bb_guard = [None, None]

        def mean_batch(oi, win, RR, CC2, nb):
            BW_ = min(BELEM // CC2, RR // P)
            BN = BW_ * CC2
            sd = mean_sdone.setdefault(oi, {})
            b = nb % 2
            if mb_guard[b]:
                nc.sync.wait_ge(S.sem, mb_guard[b])
            tok = qtern.put(ap2(mbat[b], 0, BN),
                            win[nb * BW_ * P:(nb + 1) * BW_ * P, :])
            qtern.wait(S, tok)
            BH = BN // 2
            for h in range(2):
                S.wait(S)
                S.inc(nc.scalar.activation(
                    ap2(obat, 0, BH), ap2(mbat[b], h * BH, BH), ACT.Abs,
                    accum_out=red[:, 16 * oi + 2 * nb + h:
                                  16 * oi + 2 * nb + h + 1]))
            sd[nb] = S.c
            mb_guard[b] = S.c

        def scale_of(oi, widx, RR, CC2):
            NB = nbat(RR, CC2)
            V.wait(S, mean_sdone[oi][NB - 1])
            V.wait(TE, scale_guard[0])
            V.inc(nc.vector.tensor_reduce(acc[:],
                                          red[:, 16 * oi:16 * oi + 2 * NB],
                                          AXX, ALU.add))
            V.wait(V)
            TE.wait(V)
            TE.inc(nc.tensor.matmul(psum(7, 1, 1), acc[:], onescol[:],
                                    start=True, stop=True))
            S.wait(TE)
            S.inc(nc.scalar.activation(sca[:1, 0:1], psum(7, 1, 1), ACT.Copy,
                                       scale=1.0 / float(RR * CC2)))
            V.wait(S)
            V.inc(nc.vector.tensor_scalar_max(sca[:1, 0:1], sca[:1, 0:1], 1.0e-5))
            V.wait(V)
            V.inc(nc.vector.reciprocal(sca[:1, 1:2], sca[:1, 0:1]))
            V.wait(V)
            TE.wait(V)
            TE.inc(nc.tensor.matmul(psum(7, 2), onesrow[:], sca[:1, 0:2],
                                    start=True, stop=True))
            scale_guard[0] = TE.c
            V.wait(TE)
            V.inc(nc.vector.tensor_copy(wsc[:, 2 * widx:2 * widx + 2],
                                        psum(7, 2)))
            V.wait(V)

        relayout_pend = []
        prev_tr = [None, None, None]

        def relayout_block(scr_nat, scr_T, RR, cb):
            b = cb % 3
            if prev_tr[b] is not None:
                qst.wait(nc.sync, prev_tr[b])
            tok = qtr.put(ap2(trow[b], 0, RR),
                          scr_nat[:, cb * P:(cb + 1) * P], transpose=True)
            qtr.wait(nc.gpsimd, tok)
            prev_tr[b] = qst.put(scr_T[cb * P:(cb + 1) * P, :],
                                 ap2(trow[b], 0, RR))

        for oi, (widx, win, scr_nat, scr_T, RR, CC2) in enumerate(wlist):
            NB = nbat(RR, CC2)
            if oi == 0:
                for nb in range(NB):
                    mean_batch(oi, win, RR, CC2, nb)
            nxt0 = wlist[oi + 1] if oi + 1 < len(wlist) else None
            scale_of(oi, widx, RR, CC2)
            s_w_ap = wsc[:, 2 * widx + 1:2 * widx + 2]
            BW_ = min(BELEM // CC2, RR // P)
            BN = BW_ * CC2
            NBn = nbat(nxt0[4], nxt0[5]) if nxt0 is not None else 0
            vdone = {}
            sdone2 = {}
            st_toks = {}
            pend = relayout_pend
            relayout_pend = []
            rl_per = (len(pend) + NB - 1) // NB if pend else 0
            for nb in range(NB):
                b = nb % 4
                if wb_guard[b]:
                    nc.sync.wait_ge(V.sem, wb_guard[b])
                tok = qg.put(ap2(wbat[b], 0, BN),
                             win[nb * BW_ * P:(nb + 1) * BW_ * P, :])
                qg.wait(V, tok)
                V.inc(nc.vector.tensor_scalar(ap2(qbat, 0, BN),
                                              ap2(wbat[b], 0, BN),
                                              s_w_ap, 1.49, ALU.mult, ALU.min))
                V.wait(V)
                V.inc(nc.vector.tensor_scalar(ap2(wbat[b], 0, BN),
                                              ap2(qbat, 0, BN),
                                              -1.49, RMAGIC, ALU.max, ALU.add))
                V.wait(V)
                b2 = nb % 2
                if ib_guard[b2]:
                    V.wait(S, ib_guard[b2])
                V.inc(nc.vector.tensor_scalar(ap2(ibat[b2], 0, BN),
                                              ap2(wbat[b], 0, BN),
                                              RMAGIC, None, ALU.subtract))
                vdone[nb] = V.c
                wb_guard[b] = V.c
                S.wait(V, vdone[nb])
                if bb_guard[b2] is not None:
                    qso.wait(S, bb_guard[b2])
                S.inc(nc.scalar.activation(ap2(bbat[b2], 0, BN),
                                           ap2(ibat[b2], 0, BN), ACT.Copy))
                sdone2[nb] = S.c
                ib_guard[b2] = S.c
                nc.scalar.wait_ge(S.sem, S.c)
                st_toks[nb] = qso.put(
                    scr_nat[nb * BW_ * P:(nb + 1) * BW_ * P, :],
                    ap2(bbat[b2], 0, BN))
                bb_guard[b2] = st_toks[nb]
                if nxt0 is not None and nb < NBn:
                    mean_batch(oi + 1, nxt0[1], nxt0[4], nxt0[5], nb)
                for _ in range(rl_per):
                    if pend:
                        pend.pop(0)()
            if nxt0 is not None:
                for nb2 in range(NB, NBn):
                    mean_batch(oi + 1, nxt0[1], nxt0[4], nxt0[5], nb2)
            while pend:
                pend.pop(0)()
            qso.wait_all(nc.sync)
            relayout_pend = [
                (lambda sn=scr_nat, st=scr_T, rr=RR, cb_=cb:
                 relayout_block(sn, st, rr, cb_))
                for cb in range(CC2 // P)]

        for em in relayout_pend:
            em()
        barrier()
        sctx.close()

        lnm = sbt("lnm", [P, 8])
        lnv = sbt("lnv", [P, 8])
        lnr = sbt("lnr", [P, 8])
        lnt = sbt("lnt", [P, 8])
        ln_a = sbt("ln_a", [P, 8])
        ln_b = sbt("ln_b", [P, 8])
        sA = sbt("sA", [P, NSUB])
        dsc = sbt("dsc", [P, NSUB])
        invA = sbt("invA", [P, NSUB])
        gsc = sbt("gsc", [P, NSUB])
        usc = sbt("usc", [P, NSUB])
        s2v = sbt("s2v", [P, NSUB])
        phase_n = [0]

        LW1 = LWc + 1

        def ln_tile(src_t, src_off, ns, dst_t, dst_off, sq_t):
            x3 = ap3(src_t, src_off, ns, C, C)
            d3 = ap3(dst_t, dst_off, ns, C, C)
            S.wait(V)
            V.inc(nc.vector.tensor_reduce(lnm[:, 0:ns], x3, AXX, ALU.add))
            for j in range(ns):
                S.inc(nc.scalar.activation(ap2(sq_t, j * C, C),
                                           ap2(src_t, src_off + j * C, C),
                                           ACT.Square, accum_out=lnv[:, j:j + 1]))
            V.wait(V)
            V.inc(nc.vector.tensor_scalar(lnm[:, 0:ns], lnm[:, 0:ns], 1.0 / C,
                                          None, ALU.mult))
            V.wait(V)
            V.inc(nc.vector.tensor_tensor(lnt[:, 0:ns], lnm[:, 0:ns], lnm[:, 0:ns],
                                          ALU.mult))
            V.wait(S)
            V.inc(nc.vector.tensor_scalar(lnv[:, 0:ns], lnv[:, 0:ns], 1.0 / C,
                                          1.0e-5, ALU.mult, ALU.add))
            V.wait(V)
            V.inc(nc.vector.tensor_sub(lnv[:, 0:ns], lnv[:, 0:ns], lnt[:, 0:ns]))
            V.wait(V)
            V.inc(nc.vector.reciprocal(ln_a[:, 0:ns], lnv[:, 0:ns]))
            S.wait(V)
            S.inc(nc.scalar.activation(lnt[:, 0:ns], ln_a[:, 0:ns], ACT.Sqrt))
            V.wait(S)
            V.inc(nc.vector.tensor_tensor(ln_b[:, 0:ns], lnt[:, 0:ns], lnt[:, 0:ns],
                                          ALU.mult))
            V.wait(V)
            V.inc(nc.vector.tensor_tensor(ln_b[:, 0:ns], lnv[:, 0:ns], ln_b[:, 0:ns],
                                          ALU.mult))
            V.wait(V)
            V.inc(nc.vector.tensor_scalar(ln_a[:, 0:ns], ln_b[:, 0:ns], -0.5, 1.5,
                                          ALU.mult, ALU.add))
            V.wait(V)
            V.inc(nc.vector.tensor_tensor(lnr[:, 0:ns], lnt[:, 0:ns], ln_a[:, 0:ns],
                                          ALU.mult))
            V.wait(V)
            m_bc = bass.AP(lnm, 0, [[8, P], [1, ns], [0, C]])
            r_bc = bass.AP(lnr, 0, [[8, P], [1, ns], [0, C]])
            V.inc(nc.vector.tensor_tensor(d3, x3, m_bc, ALU.subtract))
            V.wait(V)
            V.inc(nc.vector.tensor_tensor(d3, d3, r_bc, ALU.mult))
            V.wait(V)

        state = {"xqT_te": {}, "xq_war": [None, None], "hq_war": [None, None]}

        def ffn(q):
            phase_n[0] += 1
            pn = phase_n[0]
            fctx = contextlib.ExitStack()
            def fb(name, shape, dt=F32):
                return fctx.enter_context(nc.sbuf_tensor(f"{name}_{pn}", shape, dt))
            xqT = [fb(f"xqT{i}", [P, NCT, TTILE], BF16) for i in range(2)]
            gate_w = [fb(f"gate_w{i}", [P, IW]) for i in range(2)]
            u_w = [fb(f"u_w{i}", [P, IW]) for i in range(2)]
            hbuf = fb("hbuf", [P, NSUB, II], F16)
            hqb = fb("hqb", [P, II], BF16)
            xnt_ = fb("xnt_", [P, NSUB, C])
            xqb3 = fb("xqb3", [P, NSUB, C], BF16)
            xqi3 = fb("xqi3", [P, NSUB, C], I8)
            hqi = fb("hqi", [P, II], I8)
            h05b = xnt_ if NSUB * C >= II else fb("h05b", [P, II])
            gslab = [fb(f"gslab{i}", [P, NCT, IW], BF16) for i in range(2)]
            uslab = [fb(f"uslab{i}", [P, NCT, IW], BF16) for i in range(2)]
            dslab = [fb(f"dslab{i}", [P, C], BF16) for i in range(2)]
            hqT = [fb(f"hqT{i}", [P, TTILE], BF16) for i in range(2)]
            gsc2 = [fb(f"gsc2{i}", [P, NSUB]) for i in range(2)]
            usc2 = [fb(f"usc2{i}", [P, NSUB]) for i in range(2)]
            st = {"xqT_toks": {}, "hq_toks": {}, "rounds_v": {}, "gu_te": {},
                  "dn_evac": 0, "bank_evac": {}, "buf_free": {}}

            def quant(tt):
                alt = tt % 2
                sxq = scr_xq[alt]
                if state["xq_war"][alt] is not None:
                    qtr.wait(nc.scalar, state["xq_war"][alt])
                if state.get("xqb_tok") is not None:
                    qso.wait(V, state["xqb_tok"])
                    qso.wait(S, state["xqb_tok"])
                ln_tile(xres, tt * NSUB * C, NSUB, xnt_, 0, xqb3)
                V.inc(nc.vector.tensor_reduce(sA[:, 0:NSUB], xnt_[:], AXX,
                                              ALU.max, apply_absolute_value=True))
                V.wait(V)
                V.inc(nc.vector.tensor_scalar(ln_a[:, 0:NSUB], sA[:, 0:NSUB],
                                              1.0e-5, None, ALU.max))
                V.wait(V)
                V.inc(nc.vector.tensor_scalar(invA[:, 0:NSUB], ln_a[:, 0:NSUB],
                                              1.0 / 127.0, None, ALU.mult))
                V.inc(nc.vector.reciprocal(ln_b[:, 0:NSUB], ln_a[:, 0:NSUB]))
                V.wait(V)
                V.inc(nc.vector.tensor_scalar(sA[:, 0:NSUB], ln_b[:, 0:NSUB],
                                              127.0, None, ALU.mult))
                V.inc(nc.vector.tensor_tensor(gsc2[alt][:], invA[:, 0:NSUB],
                                              bass.AP(wsc, 0, [[6, P], [0, NSUB]]),
                                              ALU.mult))
                V.inc(nc.vector.tensor_tensor(usc2[alt][:], invA[:, 0:NSUB],
                                              bass.AP(wsc, 2, [[6, P], [0, NSUB]]),
                                              ALU.mult))
                V.wait(V)
                sA_bc = bass.AP(sA, 0, [[NSUB, P], [1, NSUB], [0, C]])
                V.inc(nc.vector.tensor_tensor(xnt_[:], xnt_[:], sA_bc, ALU.mult))
                V.wait(V)
                V.inc(nc.vector.tensor_scalar(xqi3[:], xnt_[:], RMAGIC, RMAGIC,
                                              ALU.add, ALU.subtract))
                V.wait(V)
                V.inc(nc.vector.tensor_copy(xqb3[:], xqi3[:]))
                nc.scalar.wait_ge(V.sem, V.c)
                tok = qso.put(
                    bass.AP(sxq.tensor, 0, [[C, P], [P * C, NSUB], [1, C]]),
                    xqb3[:])
                state["xqb_tok"] = tok
                if state["xqT_te"].get(alt):
                    nc.sync.wait_ge(TE.sem, state["xqT_te"][alt])
                qso.wait(nc.sync, tok)
                toks = []
                for ct in range(NCT):
                    toks.append(qtr.put(ap2(xqT[alt], ct * TTILE, TTILE),
                                        sxq[:, ct * P:(ct + 1) * P],
                                        transpose=True))
                state["xq_war"][alt] = toks[-1]
                st["xqT_toks"][tt] = toks

            def gu(tt):
                alt = tt % 2
                for t_ in st["xqT_toks"][tt]:
                    qtr.wait(TE, t_)
                G.wait(V, st["rounds_v"].get(tt - 1, 0))
                bank_evac = st["bank_evac"]
                buf_free = st["buf_free"]
                tern_war = {}
                for iw in range(NIW):
                    sb_ = iw % 2
                    if iw >= 2:
                        nc.sync.wait_ge(TE.sem, tern_war[iw - 2])
                    gsrc = bass.AP(scr_gT.tensor, iw * IW,
                                   [[II, P], [P * II, NCT], [1, IW]])
                    usrc = bass.AP(scr_uT.tensor, iw * IW,
                                   [[II, P], [P * II, NCT], [1, IW]])
                    tg = qtern.put(gslab[sb_][:], gsrc)
                    tu = qtern.put(uslab[sb_][:], usrc)
                    qtern.wait(TE, tg)
                    qtern.wait(TE, tu)
                    for sub in range(NSUB):
                        bk_g = 2 * sb_
                        bk_u = 2 * sb_ + 1
                        for bk in (bk_g, bk_u):
                            if bk in bank_evac:
                                e_, c_ = bank_evac[bk]
                                TE.wait(e_, c_)
                        for ct in range(NCT):
                            inst = nc.tensor.matmul(psum(bk_g, IW),
                                                    ap2(xqT[alt], ct * TTILE + sub * P, P),
                                                    ap2(gslab[sb_], ct * IW, IW),
                                                    start=(ct == 0),
                                                    stop=(ct == NCT - 1))
                        TE.inc(inst)
                        for ct in range(NCT):
                            inst = nc.tensor.matmul(psum(bk_u, IW),
                                                    ap2(xqT[alt], ct * TTILE + sub * P, P),
                                                    ap2(uslab[sb_], ct * IW, IW),
                                                    start=(ct == 0),
                                                    stop=(ct == NCT - 1))
                        te_mm = TE.inc(inst)
                        S.wait(TE, te_mm)
                        if sb_ in buf_free:
                            S.wait(G, buf_free[sb_])
                        S.inc(nc.scalar.activation(gate_w[sb_][:], psum(bk_g, IW),
                                                   ACT.Silu,
                                                   scale=gsc2[alt][:, sub:sub + 1]))
                        S.inc(nc.scalar.activation(u_w[sb_][:], psum(bk_u, IW),
                                                   ACT.Copy,
                                                   scale=usc2[alt][:, sub:sub + 1]))
                        bank_evac[bk_g] = (S, S.c - 1)
                        bank_evac[bk_u] = (S, S.c)
                        G.wait(S)
                        G.inc(nc.gpsimd.tensor_mul(
                            ap2(hbuf, sub * II + iw * IW, IW),
                            gate_w[sb_][:], u_w[sb_][:]))
                        buf_free[sb_] = G.c
                    tern_war[iw] = TE.c
                state["xqT_te"][alt] = TE.c
                st["gu_te"][tt] = TE.c

            def rounds(tt):
                alt = tt % 2
                shq = scr_hq[alt]
                if state["hq_war"][alt] is not None:
                    qtr.wait(nc.scalar, state["hq_war"][alt])
                V.wait(G)
                if state.get("hqb_tok") is not None:
                    qso.wait(V, state["hqb_tok"])
                V.inc(nc.vector.tensor_reduce(s2v[:, 0:NSUB],
                                              ap3(hbuf, 0, NSUB, II, II), AXX,
                                              ALU.max, apply_absolute_value=True))
                V.wait(V)
                V.inc(nc.vector.tensor_scalar(ln_a[:, 0:NSUB], s2v[:, 0:NSUB],
                                              1.0e-5, None, ALU.max))
                V.wait(V)
                V.inc(nc.vector.tensor_tensor(dsc[:, 0:NSUB], ln_a[:, 0:NSUB],
                                              bass.AP(wsc, 4, [[6, P], [0, NSUB]]),
                                              ALU.mult))
                V.inc(nc.vector.reciprocal(ln_b[:, 0:NSUB], ln_a[:, 0:NSUB]))
                V.wait(V)
                V.inc(nc.vector.tensor_scalar(dsc[:, 0:NSUB], dsc[:, 0:NSUB],
                                              0.5 / 127.0, None, ALU.mult))
                V.inc(nc.vector.tensor_scalar(s2v[:, 0:NSUB], ln_b[:, 0:NSUB],
                                              127.0, None, ALU.mult))
                V.wait(V)
                hq_toks = []
                for sub in range(NSUB):
                    V.wait(V)
                    V.wait(S)
                    V.inc(nc.vector.tensor_scalar(ap2(h05b, 0, II),
                                                  ap2(hbuf, sub * II, II),
                                                  s2v[:, sub:sub + 1], RMAGIC,
                                                  ALU.mult, ALU.add))
                    V.wait(V)
                    if hq_toks:
                        qso.wait(V, hq_toks[-1])
                    V.inc(nc.vector.tensor_scalar(hqi[:], ap2(h05b, 0, II),
                                                  RMAGIC, None, ALU.subtract))
                    V.wait(V)
                    V.inc(nc.vector.tensor_copy(hqb[:], hqi[:]))
                    nc.scalar.wait_ge(V.sem, V.c)
                    hq_toks.append(qso.put(shq[sub * P:(sub + 1) * P, :], hqb[:]))
                state["hqb_tok"] = hq_toks[-1]
                st["rounds_v"][tt] = V.c
                st["hq_toks"][tt] = hq_toks

            def down(tt):
                alt = tt % 2
                shq = scr_hq[alt]
                for t_ in st["hq_toks"][tt]:
                    qso.wait(nc.sync, t_)
                dn_war = {}
                last_hqT = None
                for ic in range(NIT):
                    hb = ic % 2
                    if ic >= 2:
                        nc.sync.wait_ge(TE.sem, dn_war[ic - 2])
                    thq = qtr.put(hqT[hb][:], shq[:, ic * P:(ic + 1) * P],
                                  transpose=True)
                    tds = qtern.put(dslab[hb][:], scr_dT[ic * P:(ic + 1) * P, :])
                    qtr.wait(TE, thq)
                    qtern.wait(TE, tds)
                    if ic == 0:
                        TE.wait(V, st["rounds_v"][tt])
                        if st["dn_evac"]:
                            TE.wait(V, st["dn_evac"])
                        TE.wait(G)
                    for sub in range(NSUB):
                        for cw in range(NCW):
                            inst = nc.tensor.matmul(
                                psum(sub * NCW + cw, CW),
                                ap2(hqT[hb], sub * P, P),
                                ap2(dslab[hb], cw * CW, CW),
                                start=(ic == 0), stop=(ic == NIT - 1))
                    TE.inc(inst)
                    dn_war[ic] = TE.c
                    last_hqT = thq
                state["hq_war"][alt] = last_hqT
                V.wait(TE)
                for sub in range(NSUB):
                    ti = tt * NSUB + sub
                    V.inc(nc.vector.scalar_tensor_tensor(
                        ap2(xres, ti * C, C), psum(sub * NCW, C),
                        dsc[:, sub:sub + 1], ap2(xres, ti * C, C),
                        ALU.mult, ALU.add))
                V.wait(V)
                st["dn_evac"] = V.c
                for bk in range(4):
                    st["bank_evac"][bk] = (V, V.c)

            quant(0)
            for t in range(NTT):
                gu(t)
                if t + 1 < NTT:
                    quant(t + 1)
                rounds(t)
                down(t)
            fctx.close()

        def attn(q):
            phase_n[0] += 1
            pn = phase_n[0]
            actx = contextlib.ExitStack()
            def ab(name, shape, dt=F32):
                return actx.enter_context(nc.sbuf_tensor(f"{name}_{pn}", shape, dt))
            wkT = ab("wkT", [P, NCT, C], BF16)
            wvT = ab("wvT", [P, NCT, C], BF16)
            wrT = ab("wrT", [P, NCT, C], BF16)
            woT = ab("woT", [P, NCT, C], BF16)
            xnT = ab("xnT", [P, NCT, TQP], BF16)
            lrp = ab("lrp", [P, NCT, ACH], BF16)
            rwk = ab("rwk", [P, NCT, ACH], BF16)
            tmpw = ab("tmpw", [P, NCT, LWc])
            oh = ab("oh", [P, NCT, LWc + 1])
            a2w = ab("a2w", [P, NCT, LWc])
            b2w = ab("b2w", [P, NCT, LWc])
            pw_ = ab("pw_", [P, NCT, LWc + 1])
            qw_ = ab("qw_", [P, NCT, LWc + 1])
            dw_ = ab("dw_", [P, NCT, LWc])
            aw_ = a2w
            bw_ = b2w
            nmw = ab("nmw", [P, NCT, LWc])
            dnw = tmpw
            rcw = dw_
            scw = ab("scw", [P, NCT, LWc])
            for t_ in w4_toks:
                qst.wait(nc.sync, t_)
            wt_toks = []
            for wi, wt in enumerate((wkT, wvT, wrT, woT)):
                for ct in range(NCT):
                    wt_toks.append(qtr.put(ap2(wt, ct * C, C),
                                           scr_w4[wi][:, ct * P:(ct + 1) * P],
                                           transpose=True))
            for t_ in wt_toks:
                qtr.wait(TE, t_)
            NXF = TQ // P
            lctx = contextlib.ExitStack()
            xnt_a = lctx.enter_context(
                nc.sbuf_tensor(f"xnt_a_{pn}", [P, max(NSUB, 1), C], F32))
            xqb3a = lctx.enter_context(
                nc.sbuf_tensor(f"xqb3a_{pn}", [P, max(NSUB, 1), C], BF16))
            if state.get("xqb_tok") is not None:
                qso.wait(V, state["xqb_tok"])
                qso.wait(S, state["xqb_tok"])
            xn_toks = []
            xnT_toks_h = []
            NHALF = (NXF + NSUB - 1) // NSUB
            for half in range(NHALF):
                ns = min(NSUB, NXF - half * NSUB)
                ln_tile(xres, half * NSUB * C, ns, xnt_a, 0, xqb3a)
                V.inc(nc.vector.tensor_scalar(
                    ap3(xqb3a, 0, ns, C, C), ap3(xnt_a, 0, ns, C, C),
                    1.0, None, ALU.mult))
                nc.scalar.wait_ge(V.sem, V.c)
                tok = qso.put(
                    bass.AP(scr_xn.tensor, half * NSUB * P * C,
                            [[C, P], [P * C, ns], [1, C]]),
                    ap3(xqb3a, 0, ns, C, C))
                xn_toks.append(tok)
                qso.wait(V, tok)
                qso.wait(nc.scalar, tok)
                qso.wait(nc.sync, tok)
                htoks = []
                for ct in range(NCT):
                    htoks.append(qtr.put(
                        ap2(xnT, ct * TQP + 16 + half * NSUB * P, ns * P),
                        scr_xn[half * NSUB * P:(half * NSUB + ns) * P,
                               ct * P:(ct + 1) * P],
                        transpose=True))
                xnT_toks_h.append(htoks)
            lctx.close()
            state["xqb_tok"] = None
            xkb3 = ab("xkb3", [P, 3, NCT, ACH], BF16)
            k_sl = [ab("k_sb0", [P, NCT, ACH], F16), ab("k_sb1", [P, NCT, ACH], F16)]
            v_sl = [ab("v_sb0", [P, NCT, ACH], F16), ab("v_sb1", [P, NCT, ACH], F16)]
            sr_sl = [ab("sr_sb0", [P, NCT, ACH], F16), ab("sr_sb1", [P, NCT, ACH], F16)]
            half_waited = set()

            def wait_half(h):
                if h not in half_waited:
                    half_waited.add(h)
                    for t_ in xnT_toks_h[h]:
                        qtr.wait(V, t_)
            wait_half(0)
            bdst = ap3(xnT, 15, NCT, TQP, 1)
            V.inc(nc.vector.tensor_copy(bdst, xnbound[:]))
            V.wait(V)

            wkv_vdone = {}
            sig_done = {}
            kvr_te = {}
            wo_te = {}
            lerp_v = {}
            bank_free = {}

            def kvr_lerp(ch):
                t0 = 16 + ch * ACH
                cur = ap3(xnT, t0, NCT, TQP, ACH)
                prv = ap3(xnT, t0 - 1, NCT, TQP, ACH)
                wait_half(min(NHALF - 1, ((ch + 1) * ACH - 1) // (NSUB * P)))
                V.wait(TE, kvr_te.get(ch - 1, 0))
                V.inc(nc.vector.tensor_tensor(lrp[:], cur, prv, ALU.subtract))
                V.wait(V)
                for mi, mix in enumerate((mixk, mixv, mixr)):
                    mix_b = bass.AP(mix, 0, [[NCT, P], [1, NCT], [0, ACH]])
                    xk_mi = ap3(xkb3, mi * NCT * ACH, NCT, ACH, ACH)
                    V.inc(nc.vector.tensor_tensor(xk_mi, lrp[:], mix_b, ALU.mult))
                    V.wait(V)
                    V.inc(nc.vector.tensor_tensor(xk_mi, xk_mi, prv, ALU.add))
                    V.wait(V)
                lerp_v[ch] = V.c

            kvr_pend = []

            def kvr_queue(ch):
                kvr_pend.extend((ch, mi, dch) for mi in range(3)
                                for dch in range(NCT))

            def kvr_pump(n=-1):
                while kvr_pend and n != 0:
                    n -= 1
                    ch, mi, dch = kvr_pend.pop(0)
                    slot = ch % 2
                    wt, dst = ((wkT, k_sl[slot]), (wvT, v_sl[slot]),
                               (wrT, sr_sl[slot]))[mi]
                    if mi == 0 and dch == 0:
                        TE.wait(V, lerp_v[ch])
                        S.wait(V, wkv_vdone.get(ch - 2, 0))
                    bank = 4 + (dch % 2)
                    if bank in bank_free:
                        e_, c_ = bank_free[bank]
                        TE.wait(e_, c_)
                    for ct in range(NCT):
                        inst = nc.tensor.matmul(
                            psum(bank, ACH),
                            ap2(wt, ct * C + dch * P, P),
                            ap2(xkb3, mi * NCT * ACH + ct * ACH, ACH),
                            start=(ct == 0), stop=(ct == NCT - 1))
                    TE.inc(inst)
                    S.wait(TE)
                    S.inc(nc.scalar.activation(
                        ap2(dst, dch * ACH, ACH), psum(bank, ACH),
                        ACT.Sigmoid if mi == 2 else ACT.Copy))
                    bank_free[bank] = (S, S.c)
                    if mi == 2 and dch == NCT - 1:
                        kvr_te[ch] = TE.c
                        sig_done[ch] = S.c

            kvr_lerp(0)
            kvr_queue(0)
            kvr_pump()
            NTSUB = ACH // P
            vdone_sub = {}
            wo_te2 = {}
            pair_guard = {0: 0, 1: 0}
            lws_needed = [min(NLW - 1, ((t + 1) * P - 1) // LWc)
                          for t in range(NTSUB)]

            def wo_mms(ch, tsub):
                pr = tsub % 2
                banks = (0, 1) if pr == 0 else (6, 7)
                TE.wait(V, vdone_sub[(ch, lws_needed[tsub])])
                TE.wait(V, pair_guard[pr])
                for cw in range(NCW):
                    bank = banks[cw % 2]
                    for ct in range(NCT):
                        inst = nc.tensor.matmul(
                            psum(bank, CW),
                            ap2(rwk, ct * ACH + tsub * P, P),
                            ap2(woT, ct * C + cw * CW, CW),
                            start=(ct == 0), stop=(ct == NCT - 1))
                    TE.inc(inst)
                wo_te2[(ch, tsub)] = TE.c

            def wo_evac(ch, tsub):
                pr = tsub % 2
                banks = (0, 1) if pr == 0 else (6, 7)
                ti = (ch * ACH) // P + tsub
                V.wait(TE, wo_te2[(ch, tsub)])
                for cw in range(NCW):
                    bank = banks[cw % 2]
                    V.inc(nc.vector.tensor_add(
                        ap2(xres, ti * C + cw * CW, CW),
                        ap2(xres, ti * C + cw * CW, CW),
                        psum(bank, CW)))
                pair_guard[pr] = V.c

            PUMP_N = (2 * NCT * 3 + 6 * NLW - 1) // (6 * NLW) * 2
            for ch in range(NACH):
                if ch + 1 < NACH:
                    kvr_lerp(ch + 1)
                    kvr_queue(ch + 1)
                if (ch - 1, NTSUB - 1) in wo_te2:
                    wo_evac(ch - 1, NTSUB - 1)
                k_sb = k_sl[ch % 2]
                v_sb = v_sl[ch % 2]
                sr_sb = sr_sl[ch % 2]
                V.wait(S, sig_done[ch])
                next_t = 0
                for lw in range(NLW):
                    ko = lw * LWc
                    kap = ap3(k_sb, ko, NCT, ACH, LWc)
                    vap = ap3(v_sb, ko, NCT, ACH, LWc)
                    srap = ap3(sr_sb, ko, NCT, ACH, LWc)
                    rwap = ap3(rwk, ko, NCT, ACH, LWc)
                    V.wait(V)
                    for t_ in range(ko // P, min(NTSUB - 1,
                                                 (ko + LWc - 1) // P) + 1):
                        V.wait(TE, wo_te2.get((ch - 1, t_), 0))
                    V.inc(nc.vector.tensor_tensor(tmpw[:], kap, ramp[:], ALU.add))
                    V.wait(V)
                    V.inc(nc.vector.tensor_copy(ap3(oh, 0, NCT, LW1, 1), stO[:]))
                    V.wait(V)
                    for ct in range(NCT):
                        V.inc(nc.vector.tensor_tensor_scan(
                            ap2(oh, ct * LW1 + 1, LWc), ap2(tmpw, ct * LWc, LWc),
                            ap2(tmpw, ct * LWc, LWc), stO[:, ct:ct + 1],
                            ALU.max, ALU.max))
                    V.wait(V)
                    oprev = ap3(oh, 0, NCT, LW1, LWc)
                    ocur = ap3(oh, 1, NCT, LW1, LWc)
                    V.inc(nc.vector.tensor_tensor(a2w[:], oprev, ocur, ALU.subtract))
                    V.inc(nc.vector.tensor_tensor(b2w[:], tmpw[:], ocur, ALU.subtract))
                    S.wait(V)
                    S.inc(nc.scalar.activation(a2w[:], a2w[:], ACT.Exp))
                    se1 = S.inc(nc.scalar.activation(b2w[:], b2w[:], ACT.Exp))
                    kvr_pump(PUMP_N)
                    V.wait(S, se1)
                    V.inc(nc.vector.tensor_tensor(scw[:], b2w[:], vap, ALU.mult))
                    V.inc(nc.vector.tensor_copy(ap3(pw_, 0, NCT, LW1, 1), stP[:]))
                    V.inc(nc.vector.tensor_copy(ap3(qw_, 0, NCT, LW1, 1), stQ[:]))
                    V.wait(V)
                    for ct in range(NCT):
                        V.inc(nc.vector.tensor_tensor_scan(
                            ap2(pw_, ct * LW1 + 1, LWc), ap2(a2w, ct * LWc, LWc),
                            ap2(scw, ct * LWc, LWc), stP[:, ct:ct + 1],
                            ALU.mult, ALU.add))
                        V.inc(nc.vector.tensor_tensor_scan(
                            ap2(qw_, ct * LW1 + 1, LWc), ap2(a2w, ct * LWc, LWc),
                            ap2(b2w, ct * LWc, LWc), stQ[:, ct:ct + 1],
                            ALU.mult, ALU.add))
                    V.wait(V)
                    for ct in range(NCT):
                        V.inc(nc.vector.scalar_tensor_tensor(
                            ap2(dw_, ct * LWc, LWc), ap2(oh, ct * LW1, LWc),
                            cstw[:, ct:ct + 1], ap2(tmpw, ct * LWc, LWc),
                            ALU.add, ALU.subtract))
                    V.wait(V)
                    V.inc(nc.vector.tensor_scalar_min(aw_[:], dw_[:], 0.0))
                    V.inc(nc.vector.tensor_scalar(bw_[:], dw_[:], -1.0, 0.0,
                                                  ALU.mult, ALU.min))
                    S.wait(V)
                    S.inc(nc.scalar.activation(aw_[:], aw_[:], ACT.Exp))
                    se2 = S.inc(nc.scalar.activation(bw_[:], bw_[:], ACT.Exp))
                    kvr_pump(PUMP_N)
                    pprev = ap3(pw_, 0, NCT, LW1, LWc)
                    qprev = ap3(qw_, 0, NCT, LW1, LWc)
                    V.wait(S, se2)
                    V.inc(nc.vector.tensor_tensor(dnw[:], aw_[:], qprev, ALU.mult))
                    V.wait(V)
                    V.inc(nc.vector.tensor_add(dnw[:], dnw[:], bw_[:]))
                    S.wait(V)
                    S.inc(nc.scalar.activation(rcw[:], dnw[:], ACT.Ln))
                    S.wait(S)
                    se3 = S.inc(nc.scalar.activation(rcw[:], rcw[:], ACT.Exp,
                                                     scale=-1.0))
                    kvr_pump(PUMP_N)
                    V.inc(nc.vector.tensor_tensor(nmw[:], aw_[:], pprev, ALU.mult))
                    V.wait(V)
                    V.inc(nc.vector.tensor_tensor(scw[:], bw_[:], vap, ALU.mult))
                    V.wait(V)
                    V.inc(nc.vector.tensor_add(nmw[:], nmw[:], scw[:]))
                    V.wait(V)
                    V.wait(S, se3)
                    V.inc(nc.vector.tensor_mul(nmw[:], nmw[:], rcw[:]))
                    V.wait(V)
                    V.inc(nc.vector.tensor_tensor(rwap, nmw[:], srap, ALU.mult))
                    V.inc(nc.vector.tensor_sub(stO[:], ap3(oh, LWc, NCT, LW1, 1),
                                               lwexp[:]))
                    V.inc(nc.vector.tensor_copy(stP[:], ap3(pw_, LWc, NCT, LW1, 1)))
                    V.inc(nc.vector.tensor_copy(stQ[:], ap3(qw_, LWc, NCT, LW1, 1)))
                    V.wait(V)
                    vdone_sub[(ch, lw)] = V.c
                    while next_t < NTSUB and lws_needed[next_t] <= lw:
                        wo_mms(ch, next_t)
                        next_t += 1
                kvr_pump()
                wkv_vdone[ch] = vdone_sub[(ch, NLW - 1)]
                for t_ in range(NTSUB - 1):
                    wo_evac(ch, t_)
            wo_evac(NACH - 1, NTSUB - 1)
            V.wait(V)
            blast = ap3(xnT, 16 + TQ - 1, NCT, TQP, 1)
            V.inc(nc.vector.tensor_copy(xnbound[:], blast))
            V.wait(V)
            actx.close()

        out_toks = []
        for q in range(NQ):
            if q == 0:
                tx = tx0
            else:
                xsrc = bass.AP(x_in.tensor, q * TQ * C,
                               [[C, P], [P * C, TQ // P], [1, C]])
                tx = qx.put(xres[:], xsrc)
            for E_ in (V, S, TE, G):
                qx.wait(E_, tx)
            ffn(q)
            barrier()
            attn(q)
            barrier()
            ffn(q)
            barrier()
            odst = bass.AP(out.tensor, q * TQ * C,
                           [[C, P], [P * C, TQ // P], [1, C]])
            nc.gpsimd.wait_ge(V.sem, V.c)
            ot = qst.put(odst, xres[:])
            out_toks.append(ot)
            qst.wait(nc.sync, ot)
        for t_ in out_toks:
            qst.wait(nc.gpsimd, t_)
            qst.wait(nc.scalar, t_)
    return nc


def kernel(L0_attn_x, L0_attn_w, L0_attn_u, L0_attn_mix_k, L0_attn_mix_v,
           L0_attn_mix_r, L0_attn_Wk, L0_attn_Wv, L0_attn_Wr, L0_attn_Wo,
           L0_ffn_w_gate, L0_ffn_w_up, L0_ffn_w_down):
    x = np.ascontiguousarray(np.asarray(L0_attn_x, dtype=np.float32))
    B, T, C = x.shape
    w = {
        "Wk": np.ascontiguousarray(np.asarray(L0_attn_Wk, np.float32)),
        "Wv": np.ascontiguousarray(np.asarray(L0_attn_Wv, np.float32)),
        "Wr": np.ascontiguousarray(np.asarray(L0_attn_Wr, np.float32)),
        "Wo": np.ascontiguousarray(np.asarray(L0_attn_Wo, np.float32)),
        "Wg": np.ascontiguousarray(np.asarray(L0_ffn_w_gate, np.float32)),
        "Wu": np.ascontiguousarray(np.asarray(L0_ffn_w_up, np.float32)),
        "Wd": np.ascontiguousarray(np.asarray(L0_ffn_w_down, np.float32)),
        "pw": np.ascontiguousarray(np.asarray(L0_attn_w, np.float32)),
        "pu": np.ascontiguousarray(np.asarray(L0_attn_u, np.float32)),
        "pmk": np.ascontiguousarray(np.asarray(L0_attn_mix_k, np.float32)),
        "pmv": np.ascontiguousarray(np.asarray(L0_attn_mix_v, np.float32)),
        "pmr": np.ascontiguousarray(np.asarray(L0_attn_mix_r, np.float32)),
    }
    nc = build_nc(T=T, C=C, II=int(L0_ffn_w_gate.shape[0]))
    in_maps = [dict(w, x=np.ascontiguousarray(x[b])) for b in range(B)]
    res = run_bass_kernel_spmd(nc, in_maps, core_ids=list(range(B)))
    outp = np.stack([np.asarray(res.results[b]["out"], dtype=np.float32)
                     for b in range(B)], axis=0)
    return np.ascontiguousarray(outp)
